# version 13
# baseline (speedup 1.0000x reference)
import numpy as np
import concourse.bass as bass
import concourse.mybir as mybir
from concourse.bass_utils import run_bass_kernel_spmd

F32 = mybir.dt.float32
BF16 = mybir.dt.bfloat16
ALU = mybir.AluOpType
AX = mybir.AxisListType
AF = mybir.ActivationFunctionType

D, DH = 1024, 4096
HA, DA = 16, 64
HB, DB = 8, 128
B, L = 4, 4096
NCORE, TOK = 8, 2048
W = TOK + 4
MAGIC = 12582912.0
EPS_LN = 1e-5
EPS_A = float(np.finfo(np.float32).eps)
BETA_MIN, BETA_MAX = 0.85, 0.999
ROPE_BASE = 10000.0

TILES_W = [(0, 512), (512, 512), (1024, 512), (1536, 512), (2048, 4)]
TILES_F = [(1, 510), (511, 510), (1021, 510), (1531, 510), (2041, 10)]
PH3_TILES = [(2, 512), (514, 512), (1026, 512), (1538, 512)]
WO_TILES = [(1, 512), (513, 512), (1025, 512), (1537, 512), (2049, 2)]
NCH = 16
MAX_PHASE = 9
DEBUG_TAPS = []
DEBUG_TAPS2 = []


def _host_prep(inp):
    f32 = lambda a: np.ascontiguousarray(a, dtype=np.float32)
    x = np.asarray(inp["L0_attn_a_x"], dtype=np.float32)
    per_core_x = []
    for c in range(NCORE):
        b, half = c // 2, c % 2
        s0 = half * TOK - 2
        xs = np.zeros((W, D), dtype=np.float32)
        lo, hi = max(s0, 0), min(s0 + W, L)
        xs[lo - s0: hi - s0, :] = x[b, lo:hi, :]
        per_core_x.append(f32(xs.T))
    cst = {}
    cst["wqkvT"] = f32(np.asarray(inp["L0_attn_a_w_qkv"]).T)
    cst["wprojT"] = f32(np.asarray(inp["L0_attn_a_w_proj"]).T)
    cst["w1T"] = f32(np.asarray(inp["L0_ffn_w1"]).T)
    cst["w2T"] = f32(np.asarray(inp["L0_ffn_w2"]).T)
    cw = np.asarray(inp["L0_ffn_conv_w"], dtype=np.float32)[:, 0, :]
    cwdev = np.zeros((128, 128), dtype=np.float32)
    for j in range(3):
        cwdev[:, j * 32:(j + 1) * 32] = cw[:, j].reshape(32, 128).T
    cwdev[:, 96:128] = np.asarray(inp["L0_ffn_conv_b"],
                                  dtype=np.float32).reshape(32, 128).T
    cst["convwb"] = f32(cwdev)
    cst["wqT"] = f32(np.asarray(inp["L0_attn_b_Wq"]).T)
    cst["wkT"] = f32(np.asarray(inp["L0_attn_b_Wk"]).T)
    cst["wvT"] = f32(np.asarray(inp["L0_attn_b_Wv"]).T)
    cst["woT"] = f32(np.asarray(inp["L0_attn_b_Wo"]).T)
    wba = np.concatenate([np.asarray(inp["L0_attn_b_Wbeta"]),
                          np.asarray(inp["L0_attn_b_Walpha"])], axis=0)
    cst["wbaT"] = f32(wba.T)
    cst["bba"] = f32(np.concatenate(
        [np.asarray(inp["L0_attn_b_bbeta"]),
         np.asarray(inp["L0_attn_b_balpha"])]).reshape(1, 16))
    cst["bprojF"] = f32(np.asarray(inp["L0_attn_a_b_proj"]).reshape(8, 128).T)
    cst["boF"] = f32(np.asarray(inp["L0_attn_b_bo"]).reshape(8, 128).T)
    half = DB // 2
    invf = 1.0 / ROPE_BASE ** (np.arange(half, dtype=np.float64) / half)
    freqs = np.arange(L, dtype=np.float64)[:, None] * invf[None, :]
    cos8 = np.tile(np.cos(freqs).astype(np.float32), (1, HB))
    sin8 = np.tile(np.sin(freqs).astype(np.float32), (1, HB))
    rope_c, rope_s = [], []
    for c in range(NCORE):
        s0 = (c % 2) * TOK - 2
        ct = np.zeros((W, HB * half), dtype=np.float32)
        st = np.zeros((W, HB * half), dtype=np.float32)
        lo, hi = max(s0, 0), min(s0 + W, L)
        ct[lo - s0: hi - s0] = cos8[lo:hi]
        st[lo - s0: hi - s0] = sin8[lo:hi]
        rope_c.append(f32(ct))
        rope_s.append(f32(st))
    cst["identity"] = f32(np.eye(128))
    cst["ones_col"] = f32(np.ones((128, 1)))
    cst["ones_row"] = f32(np.ones((1, 512)))
    cst["maskT"] = f32((np.arange(128)[:, None] <= np.arange(128)[None, :])
                       .astype(np.float32))
    cst["epsln"] = f32(np.full((128, 1), EPS_LN))
    jj = np.arange(128) // 8
    hh8 = np.arange(128) % 8
    cst["gamtriT"] = f32(((hh8[:, None] == hh8[None, :]) &
                          (jj[:, None] < jj[None, :])).astype(np.float32))
    em0 = np.ones((128, 1), np.float32); em0[0:2] = 0.0
    em4 = np.ones((128, 1), np.float32); em4[2:4] = 0.0
    cst["edge0"], cst["edge4"] = f32(em0), f32(em4)
    masks = []
    for c in range(NCORE):
        od = float(c % 2 == 1)
        masks.append((f32(np.full((128, 1), od)),
                      f32(np.full((128, 1), 1.0 - od))))
    return per_core_x, cst, rope_c, rope_s, masks


class Em:
    ENGINES = ("sync", "tensor", "vector", "act", "gp")

    def __init__(self, nc):
        self.nc = nc
        self.streams = {e: [] for e in self.ENGINES}
        self.sems = {}
        self.sem_val = {}
        self.waited = {e: {} for e in self.ENGINES}

    def sem(self, name):
        if name not in self.sems:
            self.sems[name] = self.nc.alloc_semaphore(name=name)
            self.sem_val[name] = 0
        return self.sems[name]

    def op(self, eng, fn, deps=(), sig=None, inc=1):
        tok = None
        if sig is not None:
            self.sem(sig)
            self.sem_val[sig] += inc
            assert self.sem_val[sig] < 64000, f"sem {sig} overflow"
            tok = (sig, self.sem_val[sig])
        waits = []
        for d in deps:
            if d is None:
                continue
            s, v = d
            w = self.waited[eng]
            if w.get(s, 0) < v:
                w[s] = v
                waits.append((self.sems[s], v))
        sem_h = self.sems[sig] if sig is not None else None

        def run(engobj, waits=waits, fn=fn, sem_h=sem_h, inc=inc):
            for (h, v) in waits:
                engobj.wait_ge(h, v)
            ins = fn(engobj)
            if sem_h is not None:
                ins.then_inc(sem_h, inc)
        self.streams[eng].append(run)
        return tok

    def emit(self, block):
        names = {"sync": "sync", "tensor": "tensor", "vector": "vector",
                 "act": "scalar", "gp": "gpsimd"}
        for eng in self.ENGINES:
            ops = self.streams[eng]
            if not ops:
                continue

            def body(engobj, ops=ops):
                for f in ops:
                    f(engobj)
            getattr(block, names[eng])(body)


def _tok(t):
    return isinstance(t, tuple) and len(t) == 2 and isinstance(t[0], str)


def _mx(*toks):
    out = []
    for t in toks:
        if t is None:
            continue
        if _tok(t):
            out.append(t)
        elif isinstance(t, (list, tuple)):
            out.extend(_mx(*t))
        else:
            raise TypeError(f"bad dep {t!r}")
    return out


class Builder:
    def __init__(self, nc, max_phase=MAX_PHASE, groups=None):
        self.groups = groups or [[0, 1], [2, 3], [4, 5], [6, 7]]
        self.nc = nc
        self.em = Em(nc)
        self.max_phase = max_phase
        self.fr_pb = [None] * 8
        self.fr = {}
        self.tok_stream = []
        self._mk_dram()

    def F(self, key):
        return self.fr.get(key)

    def S(self, key, tok):
        self.fr[key] = tok

    def _mk_dram(self):
        nc = self.nc
        self.xh = nc.dram_tensor("x", [D, W], F32, kind="ExternalInput")
        self.outh = nc.dram_tensor("out", [D, TOK], F32, kind="ExternalOutput")
        self.dh = {}
        for n, shp in [("wqkvT", [D, 3 * D]), ("wprojT", [D, D]),
                       ("w1T", [D, DH]), ("w2T", [DH, D]),
                       ("convwb", [128, 128]), ("wqT", [D, D]),
                       ("wkT", [D, D]), ("wvT", [D, D]), ("woT", [D, D]),
                       ("wbaT", [D, 16]), ("bba", [1, 16]),
                       ("bprojF", [128, 8]), ("boF", [128, 8]),
                       ("ropec", [W, 512]), ("ropes", [W, 512]),
                       ("identity", [128, 128]), ("ones_col", [128, 1]),
                       ("ones_row", [1, 512]), ("maskT", [128, 128]),
                       ("edge0", [128, 1]), ("edge4", [128, 1]),
                       ("epsln", [128, 1]), ("gamtriT", [128, 128]),
                       ("m_odd", [128, 1]), ("m_even", [128, 1])]:
            self.dh[n] = nc.dram_tensor(n, shp, F32, kind="ExternalInput")
        self.t1h = nc.dram_tensor("t1q", [32 * 128, 8 * 128], BF16, kind="Internal")
        self.t2h = nc.dram_tensor("t2q", [DH, D], BF16, kind="Internal")
        self.qstash = nc.dram_tensor("qstash", [D, W], F32, kind="Internal")
        self.xnstash = nc.dram_tensor("xnstash", [D, W], F32, kind="Internal")
        self.qsb = nc.dram_tensor("qsb", [W, D], F32, kind="Internal")
        self.ksb = nc.dram_tensor("ksb", [W, D], F32, kind="Internal")
        self.vsb = nc.dram_tensor("vsb", [W, D], F32, kind="Internal")
        self.qTst = nc.dram_tensor("qTst", [NCH * 128, HB * 128], F32,
                                   kind="Internal")
        self.ost = nc.dram_tensor("ost", [NCH * 128, HB * 128], F32,
                                  kind="Internal")
        self.ofmst = nc.dram_tensor("ofmst", [D, W], F32, kind="Internal")
        self.oTM = nc.dram_tensor("oTM", [W, D], F32, kind="Internal")
        self.cc_a_in = nc.dram_tensor("cc_a_in", [128, 520], F32, kind="Internal")
        self.cc_a_out = nc.dram_tensor("cc_a_out", [128, 520], F32, kind="Internal")
        self.cc_b_in = nc.dram_tensor("cc_b_in", [128, 1024], F32, kind="Internal")
        self.cc_b_out = nc.dram_tensor("cc_b_out", [128, 1024], F32, kind="Internal")
        self.dbg = {n: nc.dram_tensor("dbg_" + n, [D, W], F32,
                                      kind="ExternalOutput")
                    for n in DEBUG_TAPS}
        if "P3C" in DEBUG_TAPS2:
            self.dbgk = nc.dram_tensor("dbgk", [128, 128], F32,
                                       kind="ExternalOutput")
            self.dbgo = nc.dram_tensor("dbgo", [128, 128], F32,
                                       kind="ExternalOutput")

    def alloc_sbuf(self, es):
        nc = self.nc
        sb = lambda n, shp, dt=F32: es.enter_context(nc.sbuf_tensor(n, shp, dt))
        self.xres = sb("xres", [128, 8 * W])
        self.xn = sb("xn", [128, 8 * 512])
        self.wbufA = sb("wbufA", [128, 8 * 512])
        self.scr = sb("scr", [128, 4 * 512])
        self.scr2 = sb("scr2", [128, 2 * 512])
        self.bc1 = sb("bc1", [128, 512])
        self.bc2 = sb("bc2", [128, 512])
        self.n1t = sb("n1t", [128, 8 * 512], BF16)
        self.n2c = sb("n2c", [128, 512], BF16)
        self.w2row = sb("w2row", [128, 1024], BF16)
        self.t1b = sb("t1b", [128, 3072], BF16)
        self.hsb = sb("hsb", [128, 1024])
        self.rmaxb = sb("rmaxb", [128, 512])
        self.consts = sb("constsb", [128, 268])
        self.browbuf = sb("browbuf", [1, 16])
        self.onesr = sb("onesr", [1, 128])
        self.cwb = sb("cwb", [128, 128])
        self.gmt = sb("gmt", [128, 128])
        self.bF = sb("bF", [128, 16])
        self.statr = sb("statr", [1, 1536])
        self.dcol = sb("dcol", [128, 8])
        self.arena = sb("arena", [128, 32 * 512])
        self.pb = [es.enter_context(nc.psum_tensor(f"pb{i}", [128, 512], F32))
                   for i in range(8)]
        c = self.consts
        self.IDEN = c[:, 0:128]
        self.MASKT = c[:, 128:256]
        self.ONESC = c[:, 256:257]
        self.MODD = c[:, 257:258]
        self.MEVEN = c[:, 258:259]
        self.S1C = c[:, 259:260]
        self.S2C = c[:, 260:261]
        self.R1C = c[:, 261:262]
        self.R2C = c[:, 262:263]
        self.EDGE0 = c[:, 263:264]
        self.EDGE4 = c[:, 264:265]
        self.EPSC = c[:, 265:266]

    DMA_RING = 32

    def dma(self, dst, src, deps=(), sig=None, slow=False, lane=None):
        if lane == "w":
            i = getattr(self, "_dma_w", 0)
            self._dma_w = i + 1
            ring, nring, pref, q = i % 8, 8, "w", "gp"
        elif lane == "y":
            i = getattr(self, "_dma_y", 0)
            self._dma_y = i + 1
            ring, nring, pref, q = i % 4, 4, "y", "act"
        else:
            i = getattr(self, "_dma_i", 0)
            self._dma_i = i + 1
            ring, nring = i % self.DMA_RING, self.DMA_RING
            pref, q = "d", "sync"
        prev = (f"{pref}{ring}", 16 * (i // nring)) if i >= nring else None
        return self.em.op(q,
                          lambda e, dd=dst, ss=src, sl=slow: e.dma_start(
                              dd, ss, allow_slow_non_contiguous=sl),
                          deps=_mx(list(deps), prev), sig=f"{pref}{ring}",
                          inc=16)

    def mm(self, out, lhsT, rhs, start, stop, deps=(), sig="t", sgc=False):
        return self.em.op("tensor",
                          lambda e, o=out, l=lhsT, r=rhs, s1=start, s2=stop,
                          g=sgc: e.matmul(o, l, r, start=s1, stop=s2,
                                          skip_group_check=g), deps=deps,
                          sig=sig)

    def tr(self, out, in_, deps=(), sig="t", par=128):
        return self.em.op("tensor",
                          lambda e, o=out, i=in_, p=par:
                          e.transpose(o, i, self.IDEN[0:p, 0:p]),
                          deps=deps, sig=sig)

    def v(self, fn, deps=(), sig="v"):
        return self.em.op("vector", fn, deps=deps, sig=sig)

    def a(self, fn, deps=(), sig="a"):
        return self.em.op("act", fn, deps=deps, sig=sig)

    def g(self, fn, deps=(), sig="g"):
        return self.em.op("gp", fn, deps=deps, sig=sig)

    def phase0(self):
        dh = self.dh
        self.t_lib = None
        tx = [self.dma(self.xres[:, cc * W:(cc + 1) * W],
                       self.xh[cc * 128:(cc + 1) * 128, :]) for cc in range(8)]
        c = self.consts
        tc = [self.dma(c[:, 0:128], dh["identity"][:, :]),
              self.dma(c[:, 128:256], dh["maskT"][:, :]),
              self.dma(c[:, 256:257], dh["ones_col"][:, :]),
              self.dma(c[:, 257:258], dh["m_odd"][:, :]),
              self.dma(c[:, 258:259], dh["m_even"][:, :]),
              self.dma(c[:, 263:264], dh["edge0"][:, :]),
              self.dma(c[:, 264:265], dh["edge4"][:, :]),
              self.dma(c[:, 265:266], dh["epsln"][:, :]),
              self.dma(self.onesr[:, :], dh["ones_row"][0:1, 0:128]),
              self.dma(self.cwb[:, :], dh["convwb"][:, :]),
              self.dma(self.gmt[:, :], dh["gamtriT"][:, :]),
              self.dma(self.bF[:, 0:8], dh["bprojF"][:, :]),
              self.dma(self.bF[:, 8:16], dh["boF"][:, :]),
              self.dma(self.browbuf[:, :], dh["bba"][:, :])]
        self.tok_const = tc
        self.tok_x = tx
        td1 = self._quant_w(dh["w1T"], 8, DH, self.S1C, self.R1C, True)
        td2 = self._quant_w(dh["w2T"], 32, D, self.S2C, self.R2C, False)
        self.tok_wq = [td1, td2]
        self.tok_stream = tx
        self.S("wA", td2)
        self.S("scr2", td2)
        self.S("n2cA", td2)
        self.S("t1b0", td2)

    def _quant_w(self, srcT, kch, free, s_col, r_col, is_w1):
        ntile = free // 512
        n = kch * ntile
        prev3 = getattr(self, "_qw_last_t3", [None] * 4)
        taccs = []
        tred_prev = [None] * 8
        tacc = None
        for i in range(n):
            kc, ft = i // ntile, i % ntile
            sl = i % 8
            buf = self.wbufA[:, sl * 512:sl * 512 + 512]
            tld = self.dma(buf, srcT[kc * 128:(kc + 1) * 128,
                                     ft * 512:(ft + 1) * 512],
                           deps=_mx(tred_prev[sl], prev3[sl % 4]
                                    if i < 8 and sl < 4 else None), lane="w")
            tred = self.v(lambda e, bb=buf, o=self.scr2[:, 4 + sl:5 + sl]:
                          e.reduce_sum(o, bb, axis=AX.X,
                                       apply_absolute_value=True),
                          deps=_mx(tld, taccs[i - 8] if i >= 8 else None))
            tred_prev[sl] = tred
            if i == 0:
                tacc = self.v(lambda e: e.tensor_copy(
                    self.scr2[:, 1:2], self.scr2[:, 4:5]), deps=_mx(tred))
            else:
                tacc = self.v(lambda e, ii=self.scr2[:, 4 + sl:5 + sl]:
                              e.tensor_add(self.scr2[:, 1:2],
                                           self.scr2[:, 1:2], ii),
                              deps=_mx(tred, tacc))
            taccs.append(tacc)
        pb = self.pb
        tpm = self.mm(pb[6][0:1, 0:1], self.ONESC, self.scr2[:, 1:2],
                      True, True,
                      deps=_mx(tacc, self.tok_const, self.fr_pb[6]))
        tpe = self.v(lambda e: e.tensor_copy(self.scr2[0:1, 2:3],
                                             pb[6][0:1, 0:1]), deps=_mx(tpm))
        self.fr_pb[6] = tpe
        tbm = self.mm(pb[7][0:128, 0:1], self.onesr[0:1, 0:128],
                      self.scr2[0:1, 2:3], True, True,
                      deps=_mx(tpe, self.fr_pb[7]))
        tpar = self.v(lambda e: e.tensor_copy(self.scr2[:, 2:3],
                                              pb[7][:, 0:1]), deps=_mx(tbm))
        self.fr_pb[7] = tpar
        tm = self.v(lambda e: e.tensor_scalar(
            self.scr2[:, 3:4], self.scr2[:, 2:3], 1.0 / (kch * 128 * free),
            1e-5, ALU.mult, ALU.max), deps=_mx(tpar))
        self.v(lambda e, o=s_col: e.tensor_copy(o, self.scr2[:, 3:4]),
               deps=_mx(tm))
        trc = self.v(lambda e, o=r_col: e.reciprocal(o, self.scr2[:, 3:4]),
                     deps=_mx(tm))
        qouts = [(self.n2c, 0, 512), (self.t1b, 0, 3072),
                 (self.t1b, 512, 3072), (self.t1b, 1024, 3072)]
        t3_prev = [None] * 4
        tst_prev = getattr(self, "_qw_last_st", [None] * 4)
        tk = None
        for i in range(n):
            kc, ft = i // ntile, i % ntile
            sl = i % 4
            buf = self.wbufA[:, sl * 512:sl * 512 + 512]
            qt, qoff, qstride = qouts[sl]
            qout = qt[:, qoff:qoff + 512]
            tld = self.dma(buf, srcT[kc * 128:(kc + 1) * 128,
                                     ft * 512:(ft + 1) * 512],
                           deps=_mx(t3_prev[sl],
                                    tred_prev[sl], tred_prev[4 + sl]),
                           lane="w")
            t1 = self.a(lambda e, bb=buf: e.activation(
                bb, bb, AF.Copy, scale=r_col), deps=_mx(tld, trc))
            t2 = self.v(lambda e, bb=buf: e.tensor_scalar(
                bb, bb, MAGIC, -MAGIC, ALU.add, ALU.add), deps=_mx(t1))
            t3 = self.v(lambda e, bb=buf, q=qout: e.tensor_scalar(
                q, bb, 1.0, -1.0, ALU.min, ALU.max),
                deps=_mx(t2, tst_prev[sl]))
            t3_prev[sl] = t3
            if is_w1:
                src = bass.AP(qt, qoff, [[qstride, 128], [128, 4], [1, 128]])
                dst = bass.AP(self.t1h,
                              ft * 4 * 128 * 1024 + kc * 128,
                              [[1024, 128], [128 * 1024, 4], [1, 128]])
                tk = [self.dma(dst, src, deps=_mx(t3), lane="y")]
            else:
                tk = [self.dma(self.t2h[kc * 128:(kc + 1) * 128,
                                        ft * 512:(ft + 1) * 512],
                               qout, deps=_mx(t3), lane="y")]
            tst_prev[sl] = tk
        self._qw_last_t3 = t3_prev
        self._qw_last_st = list(tst_prev)
        return _mx(*tst_prev)

    def emit_ln(self, t0, tw, deps=(), want_absmax=False, use_stream=True):
        p_sum, p_sq = self.pb[4][0:1, 0:tw], self.pb[5][0:1, 0:tw]
        xres, xn, scr = self.xres, self.xn, self.scr
        strm = self.tok_stream if use_stream else None
        tmm = tmm2 = None
        for cc in range(8):
            xr = xres[:, cc * W + t0: cc * W + t0 + tw]
            tmm = self.mm(p_sum, self.ONESC, xr, cc == 0, cc == 7,
                          deps=_mx(deps, self.tok_const, self.fr_pb[4],
                                   strm))
        tsqd = [self.F("scr"), self.F("scr")]
        for cc in range(8):
            xr = xres[:, cc * W + t0: cc * W + t0 + tw]
            sq = scr[:, (cc % 2) * 512:(cc % 2) * 512 + tw]
            tsq = self.a(lambda e, o=sq, i=xr: e.activation(o, i, AF.Square),
                         deps=_mx(deps, tsqd[cc % 2], strm))
            tmm2 = self.mm(p_sq, self.ONESC, sq, cc == 0, cc == 7,
                           deps=_mx(tsq, self.fr_pb[5] if cc == 0 else None))
            tsqd[cc % 2] = tmm2
        st = self.statr
        m_r = st[0:1, 0:tw]
        v_r = st[0:1, 512:512 + tw]
        t_r = st[0:1, 512:512 + tw]
        g_r = st[0:1, 1024:1024 + tw]
        b_r = st[0:1, 512:512 + tw]
        t1 = self.v(lambda e: e.tensor_scalar(m_r, p_sum, 1.0 / D, None,
                                              ALU.mult), deps=_mx(tmm))
        t2 = self.v(lambda e: e.tensor_scalar(v_r, p_sq, 1.0 / D, None,
                                              ALU.mult), deps=_mx(tmm2))
        self.fr_pb[4] = t1
        self.fr_pb[5] = t2
        t3 = self.v(lambda e: e.scalar_tensor_tensor(
            g_r, m_r, -1.0, m_r, ALU.mult, ALU.mult), deps=_mx(t1))
        t4 = self.v(lambda e: e.tensor_add(v_r, v_r, g_r), deps=_mx(t2, t3))
        t5 = self.a(lambda e: e.activation(
            t_r, v_r, AF.Sqrt, bias=self.EPSC[0:1, 0:1]),
            deps=_mx(t4, self.tok_const))
        t6 = self.v(lambda e: e.reciprocal(g_r, t_r), deps=_mx(t5, t3))
        t7 = self.v(lambda e: e.scalar_tensor_tensor(
            b_r, m_r, -1.0, g_r, ALU.mult, ALU.mult), deps=_mx(t6))
        t8m = self.mm(self.pb[4][0:128, 0:tw], self.onesr[0:1, 0:128],
                      g_r, True, True, deps=_mx(t6, self.tok_const, t1))
        t8 = self.v(lambda e: e.tensor_copy(self.bc1[:, 0:tw],
                                            self.pb[4][0:128, 0:tw]),
                    deps=_mx(t8m, self.F("bc1")))
        self.fr_pb[4] = t8
        t9m = self.mm(self.pb[5][0:128, 0:tw], self.onesr[0:1, 0:128],
                      b_r, True, True, deps=_mx(t7, self.tok_const, t2))
        t9 = self.v(lambda e: e.tensor_copy(self.bc2[:, 0:tw],
                                            self.pb[5][0:128, 0:tw]),
                    deps=_mx(t9m, self.F("bc2")))
        self.fr_pb[5] = t9
        tn = []
        for cc in range(8):
            xr = xres[:, cc * W + t0: cc * W + t0 + tw]
            xo = xn[:, cc * 512: cc * 512 + tw]
            ta = self.v(lambda e, o=xo, i=xr: e.tensor_mul(
                o, i, self.bc1[:, 0:tw]), deps=_mx(t8, self.F("xn")))
            tb = self.v(lambda e, o=xo: e.tensor_add(o, o, self.bc2[:, 0:tw]),
                        deps=_mx(ta, t9))
            tn.append(tb)
        self.S("bc1", tn)
        self.S("bc2", tn)
        self.S("scr", tmm2)
        tok_amax = None
        if want_absmax:
            trd = self.v(lambda e: e.tensor_reduce(
                self.rmaxb[:, 0:tw],
                bass.AP(self.xn, 0, [[8 * 512, 128], [1, tw], [512, 8]]),
                AX.X, ALU.max, apply_absolute_value=True),
                deps=_mx(tn, self.F("rmaxb")))
            tok_amax = self._pmax_bcast(self.rmaxb, tw, self.scr, 0,
                                        _mx(trd, self.F("scr")))
            self.S("rmaxb", self.pm_readers)
            self.S("scr", self.pm_scr_readers)
            self.S("bc1", None)
        return tn, tok_amax

    def _pmax_bcast(self, src, tw, scr_buf, c0, deps):
        pb = self.pb
        self.pm_readers = []
        self.pm_scr_readers = []
        rows = []
        for ci, cs in enumerate(range(0, tw, 128)):
            cw = min(128, tw - cs)
            ttr = self.tr(pb[4][0:cw, 0:128], src[:, cs:cs + cw],
                          deps=_mx(deps, self.fr_pb[4]))
            self.pm_readers.append(ttr)
            trc = self.v(lambda e, o=scr_buf[0:cw, c0 + ci:c0 + ci + 1],
                         i=pb[4][0:cw, 0:128]:
                         e.tensor_reduce(o, i, AX.X, ALU.max),
                         deps=_mx(ttr, deps))
            self.fr_pb[4] = trc
            ttr2 = self.tr(pb[5][0:1, cs:cs + cw],
                           scr_buf[0:cw, c0 + ci:c0 + ci + 1],
                           deps=_mx(trc, self.fr_pb[5]), par=cw)
            tre = self.v(lambda e,
                         o=scr_buf[0:1, c0 + 512 + cs:c0 + 512 + cs + cw],
                         i=pb[5][0:1, cs:cs + cw]: e.tensor_copy(o, i),
                         deps=_mx(ttr2))
            self.fr_pb[5] = tre
            rows.append(tre)
            self.pm_scr_readers += [ttr2, tre]
        tbm = self.mm(pb[4][0:128, 0:tw], self.onesr[0:1, 0:128],
                      scr_buf[0:1, c0 + 512:c0 + 512 + tw], True, True,
                      deps=_mx(rows, self.tok_const, self.fr_pb[4]))
        tev = self.v(lambda e: e.tensor_copy(self.bc1[:, 0:tw],
                                             pb[4][0:128, 0:tw]),
                     deps=_mx(tbm, self.F("bc1")))
        self.fr_pb[4] = tev
        self.pm_scr_readers.append(tbm)
        return tev

    def phase1(self):
        A, dh, pb = self.arena, self.dh, self.pb
        for k in ("kvt0", "kvt1", "kvt2", "kvt3", "atttm", "attf", "qf"):
            self.S(k, _mx(self.F(k), self.tok_stream))
        kvt = [A[:, 0:2048], A[:, 2048:4096],
               A[:, 10752:12800], A[:, 12800:14848]]
        atttm = A[:, 4096:5120]
        czs = A[:, 5120:5640]
        exA = A[:, 5640:6160]
        qf = A[:, 6656:10752]
        attf = A[:, 10752:14848]
        wq = self.wbufA

        nbt = sum((tw + 127) // 128 for (_, tw) in TILES_W)
        bix = 0
        tq_stash = []
        tCstop = tKstop = None
        
        for ti, (t0, tw) in enumerate(TILES_W):
            tok_xn, _ = self.emit_ln(t0, tw)
            xn_users = []
            for ocg in range(2):
                tld = []
                for oc4 in range(4):
                    oc = ocg * 4 + oc4
                    twl = []
                    for ic in range(8):
                        twl.append(self.dma(
                            wq[:, oc4 * 1024 + ic * 128:
                               oc4 * 1024 + ic * 128 + 128],
                            dh["wqkvT"][ic * 128:(ic + 1) * 128,
                                        oc * 128:(oc + 1) * 128], lane="w",
                            deps=_mx(self.F("wA"))))
                    tld.append(twl)
                self.S("wA", None)
                grp = []
                for oc4 in range(4):
                    oc = ocg * 4 + oc4
                    tmq = None
                    for ic in range(8):
                        lhs = wq[:, oc4 * 1024 + ic * 128:
                                 oc4 * 1024 + ic * 128 + 128]
                        rhs = self.xn[:, ic * 512: ic * 512 + tw]
                        tmq = self.mm(pb[oc4][0:128, 0:tw], lhs, rhs,
                                      ic == 0, ic == 7,
                                      deps=_mx(tok_xn, tld[oc4],
                                               self.fr_pb[oc4]))
                    tev = self.a(lambda e, o=qf[:, oc * 512: oc * 512 + tw],
                                 i=pb[oc4][0:128, 0:tw]:
                                 e.activation(o, i, AF.Relu),
                                 deps=_mx(tmq, self.F("qf")))
                    self.fr_pb[oc4] = tev
                    grp.append(tev)
                    xn_users.append(tmq)
                self.S("wA", grp)
            tq_t = []
            for cc in range(8):
                tq_t.append(self.dma(
                    self.qstash[cc * 128:(cc + 1) * 128, t0:t0 + tw],
                    qf[:, cc * 512: cc * 512 + tw],
                    deps=_mx(grp)))
            tq_stash += tq_t
            self.S("qf", tq_t)
            nblk = (tw + 127) // 128
            kv_ev = {}
            for ot in range(4):
                twl = []
                for ic in range(8):
                    twl.append(self.dma(wq[:, ic * 512: ic * 512 + 512],
                                        dh["wqkvT"][ic * 128:(ic + 1) * 128,
                                                    1024 + ot * 512:
                                                    1024 + ot * 512 + 512],
                                        deps=_mx(self.F("wA")), lane="w"))
                self.S("wA", None)
                lastmm = None
                for bi in range(nblk):
                    bw = min(128, tw - bi * 128)
                    pk = 4 if ot % 2 == 0 else 5
                    tmk = None
                    for ic in range(8):
                        lhs = self.xn[:, ic * 512 + bi * 128:
                                      ic * 512 + bi * 128 + bw]
                        tmk = self.mm(pb[pk][0:bw, 0:512], lhs,
                                      wq[:, ic * 512: ic * 512 + 512],
                                      ic == 0, ic == 7,
                                      deps=_mx(tok_xn, twl, self.fr_pb[pk]))
                    xn_users.append(tmk)
                    lastmm = tmk
                    dst = kvt[bi % 4][0:bw, ot * 512: ot * 512 + 512]
                    if ot < 2:
                        tev = self.a(lambda e, o=dst, i=pb[pk][0:bw, 0:512]:
                                     e.activation(o, i, AF.Relu),
                                     deps=_mx(tmk, self.F(f"kvt{bi % 4}")))
                    else:
                        tev = self.v(lambda e, o=dst, i=pb[pk][0:bw, 0:512]:
                                     e.tensor_copy(o, i),
                                     deps=_mx(tmk, self.F(f"kvt{bi % 4}")))
                    self.fr_pb[pk] = tev
                    kv_ev.setdefault(bi, []).append(tev)
                self.S("wA", lastmm)
            for bi in range(nblk):
                bw = min(128, tw - bi * 128)
                deps_b = _mx(*kv_ev[bi])
                if ti == 0 and bi == 0:
                    tz = self.v(lambda e, bb=kvt[0]: e.tensor_scalar(
                        bb[:, 0:2048], bb[:, 0:2048], self.EDGE0, None,
                        ALU.mult), deps=deps_b)
                    deps_b = _mx(tz)
                if ti == 4:
                    tz = self.v(lambda e, bb=kvt[bi % 4], ww=bw:
                                e.tensor_scalar(bb[0:ww, 0:2048],
                                                bb[0:ww, 0:2048],
                                                self.EDGE4[0:ww, 0:1],
                                                None, ALU.mult), deps=deps_b)
                    deps_b = _mx(tz)
                first = (bix == 0)
                last = (bix == nbt - 1)
                tC = tK = None
                for h in range(HA):
                    hp, hr = h // 2, (h % 2) * 64
                    kh = kvt[bi % 4][0:bw, h * 64: h * 64 + 64]
                    vh = kvt[bi % 4][0:bw, 1024 + h * 64: 1024 + h * 64 + 64]
                    st_ = first and h < 2
                    sp_ = last and h >= HA - 2
                    tC = self.mm(pb[6][hr:hr + 64, hp * 64: hp * 64 + 64],
                                 kh, vh, st_, sp_, deps=deps_b, sgc=True)
                    tK = self.mm(pb[7][hr:hr + 64, hp:hp + 1], kh,
                                 self.ONESC[0:bw, 0:1], st_, sp_,
                                 deps=deps_b, sgc=True)
                self.S(f"kvt{bi % 4}", [tC, tK])
                bix += 1
                if last:
                    tCstop, tKstop = tC, tK
            self.S("xn", xn_users)
        self.S("attf", [tCstop, tKstop])
        te1 = self.v(lambda e: e.tensor_copy(exA[:, 0:512], pb[6][:, 0:512]),
                     deps=_mx(tCstop))
        te2 = self.v(lambda e: e.tensor_copy(exA[:, 512:520], pb[7][:, 0:8]),
                     deps=_mx(tKstop))
        self.fr_pb[6] = te1
        self.fr_pb[7] = te2
        td1 = self.dma(self.cc_a_in[:, :], exA[:, :], deps=_mx(te1, te2))
        tcc = self.g(lambda e: e.collective_compute(
            "AllReduce", ALU.add,
            replica_groups=self.groups,
            ins=[self.cc_a_in.ap().opt()], outs=[self.cc_a_out.ap().opt()]),
            deps=_mx(td1, self.t_lib), sig="cc")
        tcz = self.dma(czs[:, :], self.cc_a_out[:, :], deps=_mx(tcc))
        czbd = A[:, 14848:15888]
        tb0 = self.v(lambda e: e.memset(czbd[:, :], 0.0),
                     deps=_mx(self.F("czbd")))
        tbds = []
        for p in range(8):
            tbds.append(self.v(
                lambda e, o=czbd[0:64, p * 128: p * 128 + 64],
                i=czs[0:64, p * 64:(p + 1) * 64]: e.tensor_copy(o, i),
                deps=_mx(tb0, tcz)))
            tbds.append(self.v(
                lambda e, o=czbd[64:128, p * 128 + 64: p * 128 + 128],
                i=czs[64:128, p * 64:(p + 1) * 64]: e.tensor_copy(o, i),
                deps=_mx(tb0, tcz)))
            tbds.append(self.v(
                lambda e, o=czbd[0:64, 1024 + 2 * p: 1024 + 2 * p + 1],
                i=czs[0:64, 512 + p: 513 + p]: e.tensor_copy(o, i),
                deps=_mx(tb0, tcz)))
            tbds.append(self.v(
                lambda e, o=czbd[64:128, 1025 + 2 * p: 1026 + 2 * p],
                i=czs[64:128, 512 + p: 513 + p]: e.tensor_copy(o, i),
                deps=_mx(tb0, tcz)))
        tcz = tbds
        res_all = []
        for ti, (t0, tw) in enumerate(TILES_W):
            tqb = []
            for cc in range(8):
                tqb.append(self.dma(
                    qf[:, cc * 512: cc * 512 + tw],
                    self.qstash[cc * 128:(cc + 1) * 128, t0:t0 + tw],
                    deps=_mx(tq_stash if ti == 0 else None, self.F("qf")),
                    lane="w"))
            nblk = (tw + 127) // 128
            attf_toks = []
            qf_readers = []
            for bi in range(nblk):
                bw = min(128, tw - bi * 128)
                pkz = bi % 2
                tm2s = []
                for p in range(8):
                    qp = qf[0:128, p * 512 + bi * 128:
                            p * 512 + bi * 128 + bw]
                    tm2 = self.mm(pb[pkz][0:bw, 2 * p:2 * p + 2], qp,
                                  czbd[:, 1024 + 2 * p: 1026 + 2 * p],
                                  True, True,
                                  deps=_mx(tcz, tqb, self.fr_pb[pkz]
                                           if p == 0 else None))
                    tm2s.append(tm2)
                    qf_readers.append(tm2)
                tz1 = self.v(lambda e, ww=bw, i=pb[pkz][0:bw, 0:16]:
                             e.tensor_scalar(
                                 self.scr2[0:ww, 0:16], i,
                                 EPS_A, None, ALU.add),
                             deps=_mx(tm2s, self.F("scr2")))
                tz2 = self.v(lambda e, ww=bw: e.reciprocal(
                    self.scr2[0:ww, 16:32], self.scr2[0:ww, 0:16]),
                    deps=_mx(tz1))
                self.fr_pb[pkz] = tz1
                tz_all = []
                for h in range(HA):
                    hp, hr = h // 2, (h % 2) * 64
                    qh = qf[hr:hr + 64, hp * 512 + bi * 128:
                            hp * 512 + bi * 128 + bw]
                    bk = 2 + h % 2
                    tm1 = self.mm(pb[bk][0:bw, 0:64], qh,
                                  czs[hr:hr + 64, hp * 64: hp * 64 + 64],
                                  True, True,
                                  deps=_mx(tcz, tqb, self.fr_pb[bk]))
                    qf_readers.append(tm1)
                    tz3 = self.v(lambda e, i=pb[bk][0:bw, 0:64],
                                 ww=bw, hh=h,
                                 o=atttm[0:bw, h * 64: h * 64 + 64]:
                                 e.tensor_scalar(o, i,
                                                 self.scr2[0:ww,
                                                           16 + hh:17 + hh],
                                                 None, ALU.mult),
                                 deps=_mx(tm1, tz2, self.F("atttm")))
                    self.fr_pb[bk] = tz3
                    tz_all.append(tz3)
                self.fr_pb[2] = tz_all
                self.fr_pb[3] = tz_all
                self.S("scr2", tz_all)
                ttr_all = []
                for fc in range(8):
                    pk = 4 + (fc % 2)
                    ttr = self.tr(pb[pk][0:128, 0:bw],
                                  atttm[0:bw, fc * 128:(fc + 1) * 128],
                                  deps=_mx(tz_all, self.fr_pb[pk]), par=bw)
                    tev = self.v(lambda e,
                                 o=attf[:, fc * 512 + bi * 128:
                                        fc * 512 + bi * 128 + bw],
                                 i=pb[pk][0:128, 0:bw]: e.tensor_copy(o, i),
                                 deps=_mx(ttr, self.F("attf")))
                    self.fr_pb[pk] = tev
                    ttr_all.append(tev)
                    attf_toks.append(tev)
                self.S("atttm", ttr_all)
            self.S("attf", None)
            res_tile = []
            for oc in range(8):
                pk = 4 + (oc % 2)
                twl = []
                for fc in range(8):
                    twl.append(self.dma(wq[:, fc * 512: fc * 512 + 128],
                                        dh["wprojT"][fc * 128:(fc + 1) * 128,
                                                     oc * 128:(oc + 1) * 128],
                                        deps=_mx(self.F("wA")), lane="w"))
                self.S("wA", None)
                tmp = None
                for fc in range(8):
                    tmp = self.mm(pb[pk][0:128, 0:tw],
                                  wq[:, fc * 512: fc * 512 + 128],
                                  attf[:, fc * 512: fc * 512 + tw],
                                  fc == 0, fc == 7,
                                  deps=_mx(attf_toks, twl, self.fr_pb[pk]))
                self.S("wA", tmp)
                xslice = self.xres[:, oc * W + t0: oc * W + t0 + tw]
                tres = self.v(lambda e, o=xslice, i=pb[pk][0:128, 0:tw],
                              bc=self.bF[:, oc:oc + 1]:
                              e.scalar_tensor_tensor(o, i, bc, o,
                                                     ALU.add, ALU.add),
                              deps=_mx(tmp, self.tok_const))
                self.fr_pb[pk] = tres
                res_tile.append(tres)
            res_all += res_tile
            self.S("attf", res_tile)
            self.S("qf", qf_readers)
        self.tok_stream = res_all
        if "p1" in self.dbg:
            self._dump_dbg("p1")

    def _dump_dbg(self, name):
        toks = []
        for cc in range(8):
            toks.append(self.dma(self.dbg[name][cc * 128:(cc + 1) * 128, :],
                                 self.xres[:, cc * W:(cc + 1) * W],
                                 deps=_mx(self.tok_stream), sig="dbgd"))
        self.tok_stream = _mx(self.tok_stream, toks)

    def emit_ffn(self, tag):
        A, pb = self.arena, self.pb
        hpall = A[:, 0:16384]
        self.S("hp", _mx(self.F("hp"), self.tok_stream))
        entry = self.tok_stream
        res_all = []
        NT = len(TILES_F)

        def head(ti):
            f0, fw = TILES_F[ti]
            hw0, hww = f0 - 1, fw + 2
            tok_xn, tok_amax = self.emit_ln(hw0, hww, deps=entry,
                                            want_absmax=True,
                                            use_stream=False)
            tca = self.v(lambda e, w=hww: e.tensor_scalar(
                self.bc1[:, 0:w], self.bc1[:, 0:w], 1e-5, 1.0 / 127.0,
                ALU.max, ALU.mult), deps=_mx(tok_amax))
            tcb = self.v(lambda e, w=hww: e.reciprocal(self.bc2[:, 0:w],
                                                       self.bc1[:, 0:w]),
                         deps=_mx(tca, self.F("bc2")))
            tcs = self.v(lambda e, w=hww: e.tensor_scalar(
                self.scr2[:, 512:512 + w], self.bc1[:, 0:w], self.S1C,
                None, ALU.mult),
                deps=_mx(tca, self.F("hsc"), self.F("scr2"), self.tok_wq))
            tn1 = []
            for cc in range(8):
                tu = self.v(lambda e, w=hww,
                            i=self.xn[:, cc * 512: cc * 512 + hww],
                            o=self.scr[:, 0:hww]:
                            e.tensor_mul(o, i, self.bc2[:, 0:w]),
                            deps=_mx(tok_xn, tcb, self.F("scr")))
                tq = self.v(lambda e, i=self.scr[:, 0:hww],
                            o=self.n1t[:, cc * 512: cc * 512 + hww]:
                            e.tensor_scalar(o, i, MAGIC, -MAGIC, ALU.add,
                                            ALU.add),
                            deps=_mx(tu, self.F("n1t")))
                self.S("scr", tq)
                tn1.append(tq)
            self.S("xn", tn1)
            self.S("bc2", tn1)
            self.S("bc1", [tca, tcb, tcs])
            return dict(f0=f0, fw=fw, hww=hww, tok_xn=tok_xn,
                        tca=tca, tcb=tcb, tcs=tcs, tn1=tn1)

        def mm1(ti, s):
            f0, fw, hww = s["f0"], s["fw"], s["hww"]
            tn1, tcs = s["tn1"], s["tcs"]
            silus, mm1t, tevs = [], [], []
            for cc in range(32):
                wsl, pk, hsl = cc % 3, cc % 4, cc % 2
                t1s = self.t1b[:, wsl * 1024:(wsl + 1) * 1024]
                tl = self.dma(t1s, self.t1h[cc * 128:(cc + 1) * 128, :],
                              deps=_mx(self.F(f"t1b{wsl}"),
                                       self.tok_wq if cc < 3 else None),
                              lane="w")
                tmh = None
                for ic in range(8):
                    tmh = self.mm(pb[pk][0:128, 0:hww],
                                  t1s[:, ic * 128:(ic + 1) * 128],
                                  self.n1t[:, ic * 512: ic * 512 + hww],
                                  ic == 0, ic == 7,
                                  deps=_mx(tn1, tl, self.fr_pb[pk]))
                self.S(f"t1b{wsl}", tmh)
                mm1t.append(tmh)
                hss = self.hsb[:, hsl * 512: hsl * 512 + hww]
                tev = self.v(lambda e, w=hww, o=hss, i=pb[pk][0:128, 0:hww]:
                             e.tensor_mul(o, i, self.scr2[:, 512:512 + w]),
                             deps=_mx(tmh, tcs, self.F(f"hsb{hsl}")))
                self.fr_pb[pk] = tev
                tevs.append(tev)
                sca = self.scr[:, (2 + hsl) * 512: (2 + hsl) * 512 + fw]
                ta1 = self.a(lambda e, o=sca, i=hss[:, 0:fw],
                             sc1=self.cwb[:, cc:cc + 1]:
                             e.activation(o, i, AF.Copy, scale=sc1),
                             deps=_mx(tev, self.F(f"cscr{hsl}")))
                ta2 = self.v(lambda e, o=sca, i=hss[:, 2:2 + fw],
                             sc1=self.cwb[:, 64 + cc:65 + cc]:
                             e.scalar_tensor_tensor(o, i, sc1, o,
                                                    ALU.mult, ALU.add),
                             deps=_mx(ta1))
                ta3 = self.v(lambda e, o=sca, i=hss[:, 1:1 + fw],
                             sc1=self.cwb[:, 32 + cc:33 + cc]:
                             e.scalar_tensor_tensor(o, i, sc1, o,
                                                    ALU.mult, ALU.add),
                             deps=_mx(ta2))
                self.S(f"hsb{hsl}", ta3)
                hpd = hpall[:, cc * 512: cc * 512 + fw]
                ts1 = self.a(lambda e, o=hpd, i=sca,
                             bb=self.cwb[:, 96 + cc:97 + cc]:
                             e.activation(o, i, AF.Silu, bias=bb),
                             deps=_mx(ta3, self.F("hp")))
                self.S(f"cscr{hsl}", ts1)
                silus.append(ts1)
            self.S("n1t", mm1t)
            self.S("hsc", tevs)
            trd2 = self.v(lambda e, w=fw: e.tensor_reduce(
                self.hsb[:, 512:512 + w],
                bass.AP(self.arena, 0, [[32 * 512, 128], [1, w], [512, 32]]),
                AX.X, ALU.max, apply_absolute_value=True),
                deps=_mx(silus, self.F("hsb1")))
            s.update(trd2=trd2)

        def tail(ti, s):
            fw = s["fw"]
            tp2 = self._pmax_bcast(self.hsb[:, 512:1024], fw, self.xn, 0,
                                   _mx(s["trd2"], self.F("xn")))
            self.S("hsb1", self.pm_readers + [s["trd2"]])
            self.S("xn", self.pm_scr_readers)
            tc2a = self.v(lambda e, w=fw: e.tensor_scalar(
                self.bc1[:, 0:w], self.bc1[:, 0:w], 1e-5, 1.0 / 127.0,
                ALU.max, ALU.mult), deps=_mx(tp2))
            tc2b = self.v(lambda e, w=fw: e.reciprocal(self.hsb[:, 0:w],
                                                       self.bc1[:, 0:w]),
                          deps=_mx(tc2a, self.F("hsb0")))
            tc2s = self.v(lambda e, w=fw: e.tensor_scalar(
                self.scr2[:, 0:w], self.bc1[:, 0:w], self.S2C, None,
                ALU.mult), deps=_mx(tc2a, self.F("s2rd"), self.F("scr2")))
            self.S("bc1", [tc2b, tc2s])
            s.update(tc2b=tc2b, tc2s=tc2s)

        def mm2(ti, s, s_next):
            f0, fw = s["f0"], s["fw"]
            tc2b, tc2s = s["tc2b"], s["tc2s"]
            wmms = [None] * 8
            tn2r = []
            for cc in range(32):
                wsl = cc % 3
                wbuf = self.t1b[:, wsl * 1024:(wsl + 1) * 1024]
                wkey = f"t1b{wsl}"
                nsl = cc % 2
                nbuf = (self.n2c[:, 0:fw] if nsl == 0
                        else self.w2row[:, 0:fw])
                nkey = "n2cA" if nsl == 0 else "n2cB"
                tu = self.v(lambda e, w=fw,
                            i=hpall[:, cc * 512: cc * 512 + fw],
                            o=self.scr[:, 0:fw]:
                            e.tensor_mul(o, i, self.hsb[:, 0:w]),
                            deps=_mx(tc2b, self.F("scr")))
                tn2r.append(tu)
                tq = self.v(lambda e, i=self.scr[:, 0:fw], o=nbuf:
                            e.tensor_scalar(o, i, MAGIC, -MAGIC, ALU.add,
                                            ALU.add),
                            deps=_mx(tu, self.F(nkey)))
                self.S("scr", tq)
                tw2 = self.dma(wbuf, self.t2h[cc * 128:(cc + 1) * 128, :],
                               deps=_mx(self.F(wkey)), lane="w")
                mmcc = []
                for oc in range(8):
                    tmw = self.mm(pb[oc][0:128, 0:fw],
                                  wbuf[:, oc * 128:(oc + 1) * 128],
                                  nbuf, cc == 0, cc == 31,
                                  deps=_mx(tq, tw2,
                                           self.fr_pb[oc] if cc == 0
                                           else None))
                    mmcc.append(tmw)
                    wmms[oc] = tmw
                self.S(nkey, mmcc)
                self.S(wkey, mmcc)
            res_tile = []
            tscs = []
            last = (ti == NT - 1)
            for oc in range(8):
                sb2 = self.scr[:, (2 + oc % 2) * 512:(2 + oc % 2) * 512 + fw]
                tsc = self.v(lambda e, w=fw, i=pb[oc][0:128, 0:fw], o=sb2:
                             e.tensor_mul(o, i, self.scr2[:, 0:w]),
                             deps=_mx(wmms[oc], tc2s,
                                      self.F("cscr%d" % (oc % 2))))
                tscs.append(tsc)
                wnow = fw if last else fw - 1
                xslice = self.xres[:, oc * W + f0: oc * W + f0 + wnow]
                tres = self.v(lambda e, o=xslice, i=sb2[:, 0:wnow]:
                              e.tensor_add(o, o, i), deps=_mx(tsc))
                toks_cs = [tres]
                if not last:
                    tcp = self.v(lambda e, o=self.dcol[:, oc:oc + 1],
                                 i=sb2[:, fw - 1:fw]: e.tensor_copy(o, i),
                                 deps=_mx(tsc, self.F("dcol%d" % oc)))
                    tdef = self.v(lambda e,
                                  o=self.xres[:, oc * W + f0 + fw - 1:
                                              oc * W + f0 + fw],
                                  i=self.dcol[:, oc:oc + 1]:
                                  e.tensor_add(o, o, i),
                                  deps=_mx(tcp, s_next["tok_xn"]))
                    self.S("dcol%d" % oc, tdef)
                    res_all.append(tdef)
                    toks_cs += [tcp, tdef]
                self.S("cscr%d" % (oc % 2), toks_cs)
                self.fr_pb[oc] = tres
                res_tile.append(tres)
            res_all.extend(res_tile)
            self.S("hp", tn2r)
            self.S("hsb0", tn2r + [tc2b])
            self.S("s2rd", tscs)
            s["tscs"] = tscs

        st = head(0)
        for ti in range(NT):
            mm1(ti, st)
            st_next = head(ti + 1) if ti + 1 < NT else None
            tail(ti, st)
            mm2(ti, st, st_next)
            st = st_next
        self.S("scr2", _mx(self.F("hsc"), self.F("s2rd")))
        self.tok_stream = _mx(entry, res_all)
        if tag in self.dbg:
            self._dump_dbg(tag)

    def phase3(self):
        A, dh, pb = self.arena, self.dh, self.pb
        for k in ("qtm", "ktm", "vtm", "otmh", "qTh", "kTh", "pmh",
                  "bal", "sS0", "sS1", "p3hs", "p3co"):
            self.S(k, _mx(self.F(k), self.tok_stream))
        tok_arena0 = self.tok_stream
        qtm = A[:, 0:1024]
        ktm = A[:, 1024:2048]
        vtm = A[:, 2048:3072]
        sS = [A[:, 3072:4096], A[:, 4096:5120]]
        sIn = A[:, 5120:6144]
        bal = A[:, 6144:6432]
        bR = A[:, 6432:6560]
        cR = A[:, 6560:6688]
        icR = A[:, 6688:6816]
        cTt = A[:, 6816:6952]
        icTt = A[:, 6952:7088]
        ksc = A[:, 7088:7224]
        grep = A[:, 7224:7352]
        gtrep = A[:, 7352:7480]
        zc = A[:, 7480:7608]
        gam8 = A[:, 7608:7624]
        gexc = A[:, 7624:7640]
        grow = A[:, 7640:7768]
        gtrow = A[:, 7768:7896]
        bhal = A[:, 7896:7912]
        qTh = A[:, 7912:8040]
        kTh = A[:, 8040:8168]
        qT8s = A[:, 9320:10344]
        pmh = A[:, 8168:8296]
        otmh = A[:, 8296:9320]
        hrow = A[:, 9320:10344]
        hcol = A[:, 10344:10352]
        exB = A[:, 10352:11376]
        wq = self.wbufA

        twba = []
        for ic in range(8):
            twba.append(self.dma(wq[:, ic * 512: ic * 512 + 16],
                                 dh["wbaT"][ic * 128:(ic + 1) * 128, :],
                                 deps=_mx(self.F("wA")), lane="w"))
        self.S("wA", None)
        tba_ev = []
        txn_st = []
        for ti, (t0, tw) in enumerate(PH3_TILES + [(0, 2), (2050, 2)]):
            tok_xn, _ = self.emit_ln(t0, tw)
            tst = []
            for cc in range(8):
                tst.append(self.dma(
                    self.xnstash[cc * 128:(cc + 1) * 128, t0:t0 + tw],
                    self.xn[:, cc * 512: cc * 512 + tw], deps=_mx(tok_xn)))
            txn_st += tst
            if ti < 4:
                cols = [(4 * ti + jj, jj * 128, 128) for jj in range(4)]
            elif t0 == 0:
                cols = []
            else:
                cols = [(16, 0, 1)]
            xn_users = []
            for (slot, xoff, bw) in cols:
                tmb = None
                for ic in range(8):
                    lhs = self.xn[:, ic * 512 + xoff: ic * 512 + xoff + bw]
                    tmb = self.mm(pb[6][0:bw, 0:16], lhs,
                                  wq[:, ic * 512: ic * 512 + 16],
                                  ic == 0, False,
                                  deps=_mx(tok_xn, twba, self.fr_pb[6]))
                tmb2 = self.mm(pb[6][0:bw, 0:16], self.onesr[0:1, 0:bw],
                               self.browbuf[0:1, 0:16], False, True,
                               deps=_mx(self.tok_const))
                xn_users.append(tmb)
                tsg = self.a(lambda e, o=bal[0:bw, slot * 16:(slot + 1) * 16],
                             i=pb[6][0:bw, 0:16]:
                             e.activation(o, i, AF.Sigmoid),
                             deps=_mx(tmb, tmb2, self.F("bal")))
                self.fr_pb[6] = tsg
                tb2 = self.v(lambda e,
                             o=bal[0:bw, slot * 16: slot * 16 + 8]:
                             e.tensor_scalar(o, o, BETA_MAX - BETA_MIN,
                                             BETA_MIN, ALU.mult, ALU.add),
                             deps=_mx(tsg))
                tba_ev.append(tb2)
                tba_ev.append(tsg)
            self.S("xn", _mx(xn_users, tst))
        self.S("wA", tba_ev)
        tqkv_st = []
        for wname, dst in [("wqT", self.qsb), ("wkT", self.ksb),
                           ("wvT", self.vsb)]:
            for half in range(2):
                twl = []
                for ic in range(8):
                    twl.append(self.dma(wq[:, ic * 512: ic * 512 + 512],
                                        dh[wname][ic * 128:(ic + 1) * 128,
                                                  half * 512:
                                                  half * 512 + 512],
                                        deps=_mx(self.F("wA")), lane="w"))
                self.S("wA", None)
                lastmm = None
                for ti, (t0, tw) in enumerate(PH3_TILES + [(0, 2), (2050, 2)]):
                    txl = []
                    for cc in range(8):
                        txl.append(self.dma(
                            self.xn[:, cc * 512: cc * 512 + tw],
                            self.xnstash[cc * 128:(cc + 1) * 128, t0:t0 + tw],
                            lane="w",
                            deps=_mx(txn_st if (wname == "wqT" and half == 0)
                                     else None, self.F("xn"))))
                    xn_users = []
                    nblk = (tw + 127) // 128
                    for bi in range(nblk):
                        bw = min(128, tw - bi * 128)
                        pk = bi % 2
                        tmk = None
                        for ic in range(8):
                            lhs = self.xn[:, ic * 512 + bi * 128:
                                          ic * 512 + bi * 128 + bw]
                            tmk = self.mm(pb[pk][0:bw, 0:512], lhs,
                                          wq[:, ic * 512: ic * 512 + 512],
                                          ic == 0, ic == 7,
                                          deps=_mx(txl, twl, self.fr_pb[pk]))
                        xn_users.append(tmk)
                        lastmm = tmk
                        tev = self.v(lambda e,
                                     o=self.scr[:, pk * 512: pk * 512 + 512]
                                     [0:bw, :],
                                     i=pb[pk][0:bw, 0:512]: e.tensor_copy(o, i),
                                     deps=_mx(tmk, self.F(f"p3scr{pk}")))
                        self.fr_pb[pk] = tev
                        tstd = self.dma(
                            dst[t0 + bi * 128: t0 + bi * 128 + bw,
                                half * 512: half * 512 + 512],
                            self.scr[0:bw, pk * 512: pk * 512 + 512],
                            deps=_mx(tev))
                        self.S(f"p3scr{pk}", tstd)
                        tqkv_st.append(tstd)
                    self.S("xn", xn_users)
                self.S("wA", lastmm)
        dep_bal = tba_ev

        def bal_beta_ap():
            return bass.AP(self.arena, 6144, [[32 * 512, 128], [16, 16], [1, 8]])
        tgb = self.v(lambda e: e.tensor_copy(cR[:, :], bal_beta_ap()),
                     deps=_mx(dep_bal, tok_arena0))
        ttr1 = self.em.op("tensor", lambda e: e.transpose(
            pb[7][0:128, 0:128], cR[:, :], self.IDEN),
            deps=_mx(tgb, self.fr_pb[7]), sig="t")
        tev1 = self.v(lambda e: e.tensor_copy(bR[:, :], pb[7][:, 0:128]),
                      deps=_mx(ttr1))
        self.fr_pb[7] = tev1
        tz0 = self.v(lambda e: e.memset(zc[:, :], 0.0), deps=_mx(tok_arena0))
        tsc1 = self.v(lambda e: e.tensor_tensor_scan(
            cR[:, :], bR[:, :], zc[:, :], 1.0, ALU.mult, ALU.add),
            deps=_mx(tev1, tz0))
        tic = self.v(lambda e: e.reciprocal(icR[:, :], cR[:, :]),
                     deps=_mx(tsc1))
        ttr2 = self.tr(pb[7][0:128, 0:128], cR[:, :], deps=_mx(tsc1, tev1))
        tev2 = self.v(lambda e: e.tensor_copy(
            A[:, 6816:6816 + 128], pb[7][:, 0:128]), deps=_mx(ttr2))
        self.fr_pb[7] = tev2
        ttr3 = self.tr(pb[7][0:128, 0:128], icR[:, :], deps=_mx(tic, tev2))
        tev3 = self.v(lambda e: e.tensor_copy(
            A[:, 6952:6952 + 128], pb[7][:, 0:128]), deps=_mx(ttr3))
        self.fr_pb[7] = tev3
        tks = []
        for j in range(NCH):
            tks.append(self.v(lambda e, jj=j: e.tensor_tensor(
                ksc[:, jj * 8: jj * 8 + 8],
                bal[:, jj * 16 + 8: jj * 16 + 16],
                A[:, 6952 + jj * 8: 6952 + jj * 8 + 8], ALU.mult),
                deps=_mx(tev3, dep_bal)))
        ttr4 = self.tr(pb[7][0:1, 0:128], cR[:, 127:128], deps=_mx(tsc1, tev3))
        tev4 = self.v(lambda e: e.tensor_copy(grow[0:1, 0:128],
                                              pb[7][0:1, 0:128]),
                      deps=_mx(ttr4))
        self.fr_pb[7] = tev4
        tmg = self.mm(pb[7][0:128, 0:128], self.onesr[0:1, 0:128],
                      grow[0:1, 0:128], True, True,
                      deps=_mx(tev4, self.tok_const))
        tevg = self.v(lambda e: e.tensor_copy(grep[:, :], pb[7][:, 0:128]),
                      deps=_mx(tmg))
        self.fr_pb[7] = tevg
        tln = self.a(lambda e: e.activation(A[:, 7608:7609], cR[:, 127:128],
                                            AF.Ln), deps=_mx(tsc1, tevg))
        tmge = self.mm(pb[7][0:128, 0:1], self.gmt[:, :], A[:, 7608:7609],
                       True, True, deps=_mx(tln, self.fr_pb[7],
                                            self.tok_const))
        texp = self.a(lambda e: e.activation(A[:, 7609:7610],
                                             pb[7][:, 0:1], AF.Exp),
                      deps=_mx(tmge))
        self.fr_pb[7] = texp
        ttr5 = self.tr(pb[7][0:1, 0:128], A[:, 7609:7610],
                       deps=_mx(texp, self.fr_pb[7]))
        tevg5 = self.v(lambda e: e.tensor_copy(gtrow[0:1, 0:128],
                                               pb[7][0:1, 0:128]),
                       deps=_mx(ttr5))
        self.fr_pb[7] = tevg5
        tmg2 = self.mm(pb[7][0:128, 0:128], self.onesr[0:1, 0:128],
                       gtrow[0:1, 0:128], True, True, deps=_mx(tevg5, tevg))
        tevg2 = self.v(lambda e: e.tensor_copy(gtrep[:, :], pb[7][:, 0:128]),
                       deps=_mx(tmg2))
        self.fr_pb[7] = tevg2
        tqT_st, tO_st = [], []
        tlast_state = None
        tlast_cross = None
        for j in range(NCH):
            j0 = 2 + j * 128
            tql = self.dma(qtm[:, :], self.qsb[j0:j0 + 128, :],
                           deps=_mx(tqkv_st if j == 0 else None,
                                    self.F("qtm")), lane="w")
            tkl = self.dma(ktm[:, :], self.ksb[j0:j0 + 128, :],
                           deps=_mx(self.F("ktm")), lane="w")
            tvl = self.dma(vtm[:, :], self.vsb[j0:j0 + 128, :],
                           deps=_mx(self.F("vtm")), lane="w")
            trc_ = self.dma(self.scr2[:, 0:512], dh["ropec"][j0:j0 + 128, :],
                            deps=_mx(self.F("p3rc"), self.F("scr2")),
                            lane="w")
            trs_ = self.dma(self.scr2[:, 512:1024],
                            dh["ropes"][j0:j0 + 128, :],
                            deps=_mx(self.F("p3rs")), lane="w")
            self.S("scr2", None)
            tq_r = self._rope(0, trc_, trs_, _mx(tql))
            tk_r = self._rope(1024, trc_, trs_, _mx(tkl))
            self.S("p3rc", [tq_r, tk_r])
            self.S("p3rs", [tq_r, tk_r])
            tqs, tks2 = [], []
            for h in range(HB):
                tqs.append(self.v(lambda e, hh=h,
                                  sc1=A[:, 6816 + j * 8 + h:
                                        6816 + j * 8 + h + 1]:
                                  e.tensor_scalar(
                                      qtm[:, hh * 128:(hh + 1) * 128],
                                      qtm[:, hh * 128:(hh + 1) * 128],
                                      sc1, None, ALU.mult),
                                  deps=_mx(tq_r, tev2)))
                tks2.append(self.v(lambda e, hh=h,
                                   sc1=ksc[:, j * 8 + h: j * 8 + h + 1]:
                                   e.tensor_scalar(
                                       ktm[:, hh * 128:(hh + 1) * 128],
                                       ktm[:, hh * 128:(hh + 1) * 128],
                                       sc1, None, ALU.mult),
                                   deps=_mx(tk_r, tks)))
            o_ev = []
            tcross = []
            ttqs, ttks = [], []
            tevqs, qreads = [], []
            for h in range(HB):
                qsl = qT8s[:, h * 128:(h + 1) * 128]
                ttq = self.tr(pb[2][0:128, 0:128],
                              qtm[:, h * 128:(h + 1) * 128],
                              deps=_mx(tqs[h], self.fr_pb[2]))
                tevq = self.v(lambda e, o=qsl: e.tensor_copy(
                    o, pb[2][:, 0:128]),
                    deps=_mx(ttq, self.F("qTh") if h == 0 else None))
                self.fr_pb[2] = tevq
                ttqs.append(ttq)
                tevqs.append(tevq)
                ttk = self.tr(pb[3][0:128, 0:128],
                              ktm[:, h * 128:(h + 1) * 128],
                              deps=_mx(tks2[h], self.fr_pb[3]))
                tevk = self.v(lambda e: e.tensor_copy(kTh[:, :],
                                                      pb[3][:, 0:128]),
                              deps=_mx(ttk, self.F("kTh")))
                self.fr_pb[3] = tevk
                ttks.append(ttk)
                tmp1 = self.mm(pb[2][0:128, 0:128], kTh[:, :], qsl,
                               True, True,
                               deps=_mx(tevq, tevk, self.fr_pb[2]))
                qreads.append(tmp1)
                tmsk = self.v(lambda e: e.tensor_mul(pmh[:, :],
                                                     pb[2][:, 0:128],
                                                     self.MASKT),
                              deps=_mx(tmp1, self.F("pmh")))
                self.fr_pb[2] = tmsk
                tmo = self.mm(pb[6][0:128, 0:128], pmh[:, :],
                              vtm[:, h * 128:(h + 1) * 128], True,
                              j == 0, deps=_mx(tmsk, tvl, self.fr_pb[6]))
                if j > 0:
                    tmo = self.mm(pb[6][0:128, 0:128],
                                  qT8s[:, h * 128:(h + 1) * 128],
                                  sS[j % 2][:, h * 128:(h + 1) * 128],
                                  False, True,
                                  deps=_mx(tevq, tlast_state))
                    tcross.append(tmo)
                    qreads.append(tmo)
                tevo = self.v(lambda e, hh=h: e.tensor_copy(
                    otmh[:, hh * 128:(hh + 1) * 128], pb[6][:, 0:128]),
                    deps=_mx(tmo, self.F("otmh")))
                self.fr_pb[6] = tevo
                o_ev.append(tevo)
                tmd = self.mm(pb[7][0:128, 0:128],
                              ktm[:, h * 128:(h + 1) * 128],
                              vtm[:, h * 128:(h + 1) * 128], True, True,
                              deps=_mx(tks2[h], tvl, self.fr_pb[7]))
                gcol = grep[:, j * 8 + h: j * 8 + h + 1]
                if j == 0:
                    tst = self.v(lambda e, hh=h, gc=gcol: e.tensor_scalar(
                        sS[1][:, hh * 128:(hh + 1) * 128], pb[7][:, 0:128],
                        gc, None, ALU.mult),
                        deps=_mx(tmd, tevg, self.F("sS1")))
                else:
                    t_g = self.v(lambda e, gc=gcol: e.tensor_scalar(
                        A[:, 14848:14976], pb[7][:, 0:128], gc, None,
                        ALU.mult), deps=_mx(tmd, tevg, self.F("p3st")))
                    dst = sS[(j + 1) % 2][:, h * 128:(h + 1) * 128]
                    src = sS[j % 2][:, h * 128:(h + 1) * 128]
                    tst = self.v(lambda e, o=dst, i=src, gc=gcol:
                                 e.scalar_tensor_tensor(
                                     o, i, gc, A[:, 14848:14976],
                                     ALU.mult, ALU.add),
                                 deps=_mx(t_g, tlast_cross,
                                          self.F("sS%d" % ((j + 1) % 2))))
                    self.S("p3st", tst)
                self.fr_pb[7] = tst
                if h == HB - 1:
                    tlast_state = tst
            if tcross:
                tlast_cross = tcross
            self.S("sS%d" % ((j + 1) % 2), None)
            self.S("qtm", ttqs)
            self.S("ktm", ttks + [tmd])
            self.S("vtm", [tmd, tmo])
            tstq = self.dma(self.qTst[j * 128:(j + 1) * 128, :],
                            qT8s[:, :], deps=_mx(tevqs))
            tqT_st.append(tstq)
            self.S("qTh", qreads + [tstq])
            tso = self.dma(self.ost[j * 128:(j + 1) * 128, :], otmh[:, :],
                           deps=_mx(o_ev))
            tO_st.append(tso)
            self.S("otmh", tso)
            self.S("kTh", tmp1)
            self.S("pmh", tmo)
        thr = self._halo_right(sS[NCH % 2], bal, bhal,
                               _mx(tlast_state, tqkv_st))
        tex1 = self.v(lambda e: e.tensor_scalar(
            exB[:, :], sS[NCH % 2][:, :], self.MEVEN, None, ALU.mult),
            deps=_mx(tlast_state, self.tok_const))
        tex2 = self.dma(self.cc_b_in[:, :], exB[:, :], deps=_mx(tex1))
        tcc = self.g(lambda e: e.collective_compute(
            "AllReduce", ALU.add,
            replica_groups=self.groups,
            ins=[self.cc_b_in.ap().opt()], outs=[self.cc_b_out.ap().opt()]),
            deps=_mx(tex2, self.t_lib), sig="cc")
        tex3 = self.dma(sIn[:, :], self.cc_b_out[:, :],
                        deps=_mx(tcc, self.tok_stream))
        tex4 = self.v(lambda e: e.tensor_scalar(
            sIn[:, :], sIn[:, :], self.MODD, None, ALU.mult), deps=_mx(tex3))
        thl = self._halo_left(sIn, _mx(tex4, tqkv_st))
        tofm_st = [thr, thl]
        qT8 = [A[:, 0:1024], A[:, 1024:2048]]
        otm2 = [otmh, A[:, 2048:3072]]
        kTh8 = A[:, 3072:4096]
        s8 = A[:, 4096:5120]
        self.S("s8", _mx(tex1, thr, tO_st))
        tst_prev = _mx(tex1, thr)
        for j in range(NCH):
            j0 = 2 + j * 128
            sl3 = j % 2
            tol = self.dma(otm2[sl3][:, :],
                           self.ost[j * 128:(j + 1) * 128, :],
                           deps=_mx([tO_st, tex1, thr] if j < 2 else None,
                                    self.F(f"otm2{sl3}")), lane="w")
            tq8 = self.dma(qT8[sl3][:, :], self.qTst[j * 128:(j + 1) * 128, :],
                           deps=_mx([tqT_st, tex1, thr] if j < 2 else None,
                                    self.F(f"qT8{sl3}")), lane="w")
            corr, tevos, qcons = [], [], []
            ts8 = []
            for h in range(HB):
                ts8.append(self.v(
                    lambda e, si=sIn[:, h * 128:(h + 1) * 128],
                    gc=gtrep[:, j * 8 + h: j * 8 + h + 1],
                    o=s8[:, h * 128:(h + 1) * 128]:
                    e.tensor_scalar(o, si, gc, None, ALU.mult),
                    deps=_mx(tex4, tevg2, self.F("s8"))))
            self.S("s8", None)
            tadds = []
            for h in range(HB):
                bk = (2, 3, 6, 7)[h % 4]
                tmc = self.mm(pb[bk][0:128, 0:128],
                              qT8[sl3][:, h * 128:(h + 1) * 128],
                              s8[:, h * 128:(h + 1) * 128], True, True,
                              deps=_mx(tq8, ts8[h], self.fr_pb[bk]))
                qcons.append(tmc)
                tadd = self.v(lambda e, hh=h, o=otm2[sl3], b=bk:
                              e.tensor_add(
                                  o[:, hh * 128:(hh + 1) * 128],
                                  o[:, hh * 128:(hh + 1) * 128],
                                  pb[b][:, 0:128]), deps=_mx(tol, tmc))
                self.fr_pb[bk] = tadd
                tadds.append(tadd)
            self.S("s8", qcons)
            tsto = self.dma(self.oTM[j0:j0 + 128, :], otm2[sl3][:, :],
                            deps=_mx(tadds))
            tofm_st.append(tsto)
            self.S(f"otm2{sl3}", tadds + [tsto])
            self.S(f"qT8{sl3}", qcons)
        res_all = []
        first_wo = True
        for (t0, tw) in WO_TILES:
            tofl = []
            nb = (tw + 127) // 128
            for tb in range(nb):
                r = min(128, tw - tb * 128)
                osl = tb % 2
                obuf = A[:, osl * 1024: osl * 1024 + 1024]
                told = self.dma(obuf[0:r, :],
                                self.oTM[t0 + tb * 128: t0 + tb * 128 + r, :],
                                deps=_mx(tofm_st if first_wo else None,
                                         self.F(f"wotm{osl}")), lane="w")
                first_wo = False
                trds = []
                for fc in range(8):
                    bk = 4 + fc % 2
                    ttr = self.tr(pb[bk][0:128, 0:r],
                                  obuf[0:r, fc * 128:(fc + 1) * 128],
                                  deps=_mx(told, self.fr_pb[bk]), par=r)
                    tev = self.v(lambda e, o=self.xn[:, fc * 512 + tb * 128:
                                                     fc * 512 + tb * 128 + r],
                                 i=pb[bk][0:128, 0:r]: e.tensor_copy(o, i),
                                 deps=_mx(ttr, self.F("xn")))
                    self.fr_pb[bk] = tev
                    trds.append(ttr)
                    tofl.append(tev)
                self.S(f"wotm{osl}", trds)
            xn_users = []
            for oc in range(8):
                pk = oc % 2
                twl = []
                for fc in range(8):
                    twl.append(self.dma(wq[:, fc * 512: fc * 512 + 128],
                                        dh["woT"][fc * 128:(fc + 1) * 128,
                                                  oc * 128:(oc + 1) * 128],
                                        deps=_mx(self.F("wA")), lane="w"))
                self.S("wA", None)
                tmp = None
                for fc in range(8):
                    tmp = self.mm(pb[pk][0:128, 0:tw],
                                  wq[:, fc * 512: fc * 512 + 128],
                                  self.xn[:, fc * 512: fc * 512 + tw],
                                  fc == 0, fc == 7,
                                  deps=_mx(tofl, twl, self.fr_pb[pk]))
                self.S("wA", tmp)
                xn_users.append(tmp)
                xslice = self.xres[:, oc * W + t0: oc * W + t0 + tw]
                tres = self.v(lambda e, o=xslice, i=pb[pk][0:128, 0:tw],
                              bc=self.bF[:, 8 + oc: 9 + oc]:
                              e.scalar_tensor_tensor(o, i, bc, o,
                                                     ALU.add, ALU.add),
                              deps=_mx(tmp, self.tok_const))
                self.fr_pb[pk] = tres
                res_all.append(tres)
            self.S("xn", xn_users)
        self.tok_stream = res_all
        if "p3" in self.dbg:
            self._dump_dbg("p3")

    def _rope(self, base, tcs, tsn, deps):
        scr, scr2, Ah = self.scr, self.scr2, self.arena

        def sl(off):
            return bass.AP(Ah, base + off, [[32 * 512, 128], [128, 8], [2, 64]])

        def sc(off):
            return bass.AP(scr, off, [[4 * 512, 128], [64, 8], [1, 64]])
        cosap = bass.AP(scr2, 0, [[2 * 512, 128], [64, 8], [1, 64]])
        sinap = bass.AP(scr2, 512, [[2 * 512, 128], [64, 8], [1, 64]])
        t1 = self.v(lambda e: e.tensor_tensor(sc(0), sl(0), cosap, ALU.mult),
                    deps=_mx(deps, tcs, self.F("scr")))
        t2 = self.v(lambda e: e.tensor_tensor(sc(512), sl(1), sinap, ALU.mult),
                    deps=_mx(deps, tsn))
        t3 = self.v(lambda e: e.tensor_tensor(sc(1024), sl(1), cosap, ALU.mult),
                    deps=_mx(deps, tcs))
        t4 = self.v(lambda e: e.tensor_tensor(sc(1536), sl(0), sinap, ALU.mult),
                    deps=_mx(deps, tsn))
        t5 = self.v(lambda e: e.tensor_tensor(sl(0), sc(0), sc(512),
                                              ALU.subtract),
                    deps=_mx(t1, t2, t4))
        t6 = self.v(lambda e: e.tensor_tensor(sl(1), sc(1024), sc(1536),
                                              ALU.add), deps=_mx(t3, t5))
        self.S("scr", t6)
        return t6

    def _halo_right(self, s16, bal, bhal, deps):
        A, pb = self.arena, self.pb
        QR, KR, VR, OH = 11376, 12400, 13424, 14448
        tmb = self.mm(pb[3][0:128, 0:16], self.onesr[0:1, 0:128],
                      bal[0:1, 256:272], True, True,
                      deps=_mx(deps, self.fr_pb[3]))
        tbh = self.v(lambda e: e.tensor_copy(bhal[:, :], pb[3][:, 0:16]),
                     deps=_mx(tmb))
        self.fr_pb[3] = tbh
        tq = self.dma(A[0:1, QR:QR + 1024], self.qsb[2050:2051, :],
                      deps=deps, lane="w")
        tk = self.dma(A[0:1, KR:KR + 1024], self.ksb[2050:2051, :],
                      deps=deps, lane="w")
        tv = self.dma(A[0:1, VR:VR + 1024], self.vsb[2050:2051, :],
                      deps=deps, lane="w")
        trc_ = self.dma(self.scr2[0:1, 0:512], self.dh["ropec"][2050:2051, :],
                        deps=_mx(self.F("p3rc")))
        trs_ = self.dma(self.scr2[0:1, 512:1024],
                        self.dh["ropes"][2050:2051, :],
                        deps=_mx(self.F("p3rs")))
        tqr = self._rope_row(QR, trc_, trs_, _mx(tq))
        tkr = self._rope_row(KR, trc_, trs_, _mx(tk))
        self.S("p3rc", [tqr, tkr])
        self.S("p3rs", [tqr, tkr])
        tqc = []
        for h in range(HB):
            ttc = self.em.op("tensor", lambda e, hh=h: e.transpose(
                pb[3][0:128, hh:hh + 1],
                bass.AP(self.arena, QR + hh * 128, [[32 * 512, 1], [1, 128]]),
                self.IDEN[0:1, 0:1]),
                deps=_mx(tqr, self.fr_pb[3] if h == 0 else None), sig="t")
        tqc = self.v(lambda e: e.tensor_copy(A[:, OH:OH + 8], pb[3][:, 0:8]),
                     deps=_mx(ttc))
        self.fr_pb[3] = tqc
        o_mms = []
        tprev = None
        for h in range(HB):
            tkv = self.mm(pb[5][0:128, 0:128],
                          bass.AP(self.arena, KR + h * 128,
                                  [[32 * 512, 1], [1, 128]]),
                          bass.AP(self.arena, VR + h * 128,
                                  [[32 * 512, 1], [1, 128]]),
                          True, True, deps=_mx(tkr, tv, self.fr_pb[5]))
            ta = self.v(lambda e, hh=h: e.tensor_scalar(
                A[:, 15104:15232], pb[5][:, 0:128],
                bhal[:, 8 + hh: 9 + hh], None, ALU.mult),
                deps=_mx(tkv, tbh, self.F("p3hs")))
            tb = self.v(lambda e, hh=h: e.scalar_tensor_tensor(
                A[:, 15232:15360], s16[:, hh * 128:(hh + 1) * 128],
                bhal[:, hh: hh + 1], A[:, 15104:15232],
                ALU.mult, ALU.add), deps=_mx(ta, deps))
            self.fr_pb[5] = tb
            self.S("p3hs", tb)
            tmo = self.mm(pb[4][0:1, (h % 4) * 128:(h % 4) * 128 + 128],
                          A[:, OH + h: OH + h + 1], A[:, 15232:15360],
                          True, True,
                          deps=_mx(tb, tqc, self.fr_pb[4]))
            tevo = self.v(lambda e, hh=h:
                          e.tensor_copy(A[0:1, 15360 + hh * 128:
                                          15360 + (hh + 1) * 128],
                                        pb[4][0:1,
                                              (hh % 4) * 128:
                                              (hh % 4) * 128 + 128]),
                          deps=_mx(tmo, self.F("p3or")))
            self.fr_pb[4] = tevo
            o_mms.append(tevo)
        tst = self.dma(self.oTM[2050:2051, :], A[0:1, 15360:16384],
                       deps=_mx(o_mms))
        self.S("p3or", tst)
        self.S("p3oh", tst)
        return tst

    def _halo_left(self, sIn, deps):
        A, pb = self.arena, self.pb
        QR, OH = 11376, 14448
        tq = self.dma(A[0:1, QR:QR + 1024], self.qsb[1:2, :],
                      deps=_mx(deps, self.F("p3qr")), lane="w")
        trc_ = self.dma(self.scr2[0:1, 0:512], self.dh["ropec"][1:2, :],
                        deps=_mx(self.F("p3rc")))
        trs_ = self.dma(self.scr2[0:1, 512:1024], self.dh["ropes"][1:2, :],
                        deps=_mx(self.F("p3rs")))
        tqr = self._rope_row(QR, trc_, trs_, _mx(tq))
        self.S("p3rc", tqr)
        self.S("p3rs", tqr)
        tqc = []
        for h in range(HB):
            ttc = self.em.op("tensor", lambda e, hh=h: e.transpose(
                pb[3][0:128, hh:hh + 1],
                bass.AP(self.arena, QR + hh * 128, [[32 * 512, 1], [1, 128]]),
                self.IDEN[0:1, 0:1]),
                deps=_mx(tqr, self.fr_pb[3] if h == 0 else None), sig="t")
        tqc = self.v(lambda e: e.tensor_copy(A[:, OH:OH + 8], pb[3][:, 0:8]),
                     deps=_mx(ttc, self.F("p3qc")))
        self.fr_pb[3] = tqc
        o_mms = []
        for h in range(HB):
            tmo = self.mm(pb[4][0:1, (h % 4) * 128:(h % 4) * 128 + 128],
                          A[:, OH + h: OH + h + 1],
                          sIn[:, h * 128:(h + 1) * 128],
                          True, True,
                          deps=_mx(deps, tqc, self.fr_pb[4]))
            tevo = self.v(lambda e, hh=h:
                          e.tensor_copy(A[0:1, 15360 + hh * 128:
                                          15360 + (hh + 1) * 128],
                                        pb[4][0:1,
                                              (hh % 4) * 128:
                                              (hh % 4) * 128 + 128]),
                          deps=_mx(tmo, self.F("p3or")))
            self.fr_pb[4] = tevo
            o_mms.append(tevo)
        tst = self.dma(self.oTM[1:2, :], A[0:1, 15360:16384],
                       deps=_mx(o_mms))
        self.S("p3or", tst)
        return tst

    def _rope_row(self, base, tcs, tsn, deps):
        scr, scr2 = self.scr, self.scr2

        def sl(off):
            return bass.AP(self.arena, base + off,
                           [[32 * 512, 1], [128, 8], [2, 64]])

        def sc(off):
            return bass.AP(scr, off, [[4 * 512, 1], [64, 8], [1, 64]])
        cosap = bass.AP(scr2, 0, [[2 * 512, 1], [64, 8], [1, 64]])
        sinap = bass.AP(scr2, 512, [[2 * 512, 1], [64, 8], [1, 64]])
        t1 = self.v(lambda e: e.tensor_tensor(sc(0), sl(0), cosap, ALU.mult),
                    deps=_mx(deps, tcs, self.F("scr")))
        t2 = self.v(lambda e: e.tensor_tensor(sc(512), sl(1), sinap, ALU.mult),
                    deps=_mx(deps, tsn))
        t3 = self.v(lambda e: e.tensor_tensor(sc(1024), sl(1), cosap,
                                              ALU.mult), deps=_mx(deps, tcs))
        t4 = self.v(lambda e: e.tensor_tensor(sc(1536), sl(0), sinap,
                                              ALU.mult), deps=_mx(deps, tsn))
        t5 = self.v(lambda e: e.tensor_tensor(sl(0), sc(0), sc(512),
                                              ALU.subtract),
                    deps=_mx(t1, t2, t4))
        t6 = self.v(lambda e: e.tensor_tensor(sl(1), sc(1024), sc(1536),
                                              ALU.add), deps=_mx(t3, t5))
        self.S("scr", t6)
        return t6

    def output(self):
        toks = []
        for cc in range(8):
            toks.append(self.dma(
                self.outh[cc * 128:(cc + 1) * 128, :],
                self.xres[:, cc * W + 2: cc * W + 2 + TOK],
                deps=_mx(self.tok_stream), sig="dout"))
        self.em.op("sync", lambda e: e.nop(), deps=_mx(toks))

    def build(self):
        from contextlib import ExitStack
        es = ExitStack()
        with es:
            self.alloc_sbuf(es)
            self.phase0()
            if self.max_phase >= 1:
                self.phase1()
            if self.max_phase >= 2:
                self.emit_ffn("p2")
            if self.max_phase >= 3:
                self.phase3()
            if self.max_phase >= 4:
                self.emit_ffn("p4")
            self.output()
            final = [(n, v) for n, v in self.em.sem_val.items()
                     if n.startswith(("d", "w", "y"))]
            self.em.op("sync", lambda e: e.nop(), deps=final)
            with self.nc.Block() as block:
                self.em.emit(block)
        mybir.codegen_inst_isa_subclasses(self.nc)
        return self.nc


def build_program(max_phase=MAX_PHASE, groups=None):
    nc = bass.Bass("TRN2", target_bir_lowering=False)
    return Builder(nc, max_phase, groups).build()


def kernel(**inputs):
    per_core_x, cst, rope_c, rope_s, masks = _host_prep(inputs)
    nc = build_program()
    in_maps = []
    for c in range(NCORE):
        m = {"x": per_core_x[c], "ropec": rope_c[c], "ropes": rope_s[c],
             "m_odd": masks[c][0], "m_even": masks[c][1]}
        m.update(cst)
        in_maps.append(m)
    res = run_bass_kernel_spmd(nc, in_maps, core_ids=list(range(NCORE)))
    out = np.zeros((B, L, D), dtype=np.float32)
    for c in range(NCORE):
        b, half = c // 2, c % 2
        out[b, half * TOK:(half + 1) * TOK, :] = \
            np.asarray(res.results[c]["out"]).T
    return out


# version 14 (final)
# speedup vs baseline: 1.0829x; 1.0829x over previous
import numpy as np
import concourse.bass as bass
import concourse.mybir as mybir
from concourse.bass_utils import run_bass_kernel_spmd

F32 = mybir.dt.float32
BF16 = mybir.dt.bfloat16
ALU = mybir.AluOpType
AX = mybir.AxisListType
AF = mybir.ActivationFunctionType

D, DH = 1024, 4096
HA, DA = 16, 64
HB, DB = 8, 128
B, L = 4, 4096
NCORE, TOK = 8, 2048
W = TOK + 4
MAGIC = 12582912.0
EPS_LN = 1e-5
EPS_A = float(np.finfo(np.float32).eps)
BETA_MIN, BETA_MAX = 0.85, 0.999
ROPE_BASE = 10000.0

TILES_W = [(0, 512), (512, 512), (1024, 512), (1536, 512), (2048, 4)]
TILES_F = [(1, 510), (511, 510), (1021, 510), (1531, 510), (2041, 10)]
PH3_TILES = [(2, 512), (514, 512), (1026, 512), (1538, 512)]
WO_TILES = [(1, 512), (513, 512), (1025, 512), (1537, 512), (2049, 2)]
NCH = 16
MAX_PHASE = 9
DEBUG_TAPS = []
DEBUG_TAPS2 = []


def _host_prep(inp):
    f32 = lambda a: np.ascontiguousarray(a, dtype=np.float32)
    x = np.asarray(inp["L0_attn_a_x"], dtype=np.float32)
    per_core_x = []
    for c in range(NCORE):
        b, half = c // 2, c % 2
        s0 = half * TOK - 2
        xs = np.zeros((W, D), dtype=np.float32)
        lo, hi = max(s0, 0), min(s0 + W, L)
        xs[lo - s0: hi - s0, :] = x[b, lo:hi, :]
        per_core_x.append(f32(xs.T))
    cst = {}
    cst["wqkvT"] = f32(np.asarray(inp["L0_attn_a_w_qkv"]).T)
    cst["wprojT"] = f32(np.asarray(inp["L0_attn_a_w_proj"]).T)
    cst["w1T"] = f32(np.asarray(inp["L0_ffn_w1"]).T)
    cst["w2T"] = f32(np.asarray(inp["L0_ffn_w2"]).T)
    cw = np.asarray(inp["L0_ffn_conv_w"], dtype=np.float32)[:, 0, :]
    cwdev = np.zeros((128, 128), dtype=np.float32)
    for j in range(3):
        cwdev[:, j * 32:(j + 1) * 32] = cw[:, j].reshape(32, 128).T
    cwdev[:, 96:128] = np.asarray(inp["L0_ffn_conv_b"],
                                  dtype=np.float32).reshape(32, 128).T
    cst["convwb"] = f32(cwdev)
    cst["wqT"] = f32(np.asarray(inp["L0_attn_b_Wq"]).T)
    cst["wkT"] = f32(np.asarray(inp["L0_attn_b_Wk"]).T)
    cst["wvT"] = f32(np.asarray(inp["L0_attn_b_Wv"]).T)
    cst["woT"] = f32(np.asarray(inp["L0_attn_b_Wo"]).T)
    wba = np.concatenate([np.asarray(inp["L0_attn_b_Wbeta"]),
                          np.asarray(inp["L0_attn_b_Walpha"])], axis=0)
    cst["wbaT"] = f32(wba.T)
    cst["bba"] = f32(np.concatenate(
        [np.asarray(inp["L0_attn_b_bbeta"]),
         np.asarray(inp["L0_attn_b_balpha"])]).reshape(1, 16))
    cst["bprojF"] = f32(np.asarray(inp["L0_attn_a_b_proj"]).reshape(8, 128).T)
    cst["boF"] = f32(np.asarray(inp["L0_attn_b_bo"]).reshape(8, 128).T)
    half = DB // 2
    invf = 1.0 / ROPE_BASE ** (np.arange(half, dtype=np.float64) / half)
    freqs = np.arange(L, dtype=np.float64)[:, None] * invf[None, :]
    cos8 = np.tile(np.cos(freqs).astype(np.float32), (1, HB))
    sin8 = np.tile(np.sin(freqs).astype(np.float32), (1, HB))
    rope_c, rope_s = [], []
    for c in range(NCORE):
        s0 = (c % 2) * TOK - 2
        ct = np.zeros((W, HB * half), dtype=np.float32)
        st = np.zeros((W, HB * half), dtype=np.float32)
        lo, hi = max(s0, 0), min(s0 + W, L)
        ct[lo - s0: hi - s0] = cos8[lo:hi]
        st[lo - s0: hi - s0] = sin8[lo:hi]
        rope_c.append(f32(ct))
        rope_s.append(f32(st))
    cst["identity"] = f32(np.eye(128))
    cst["ones_col"] = f32(np.ones((128, 1)))
    cst["ones_row"] = f32(np.ones((1, 512)))
    cst["maskT"] = f32((np.arange(128)[:, None] <= np.arange(128)[None, :])
                       .astype(np.float32))
    cst["epsln"] = f32(np.full((128, 1), EPS_LN))
    jj = np.arange(128) // 8
    hh8 = np.arange(128) % 8
    cst["gamtriT"] = f32(((hh8[:, None] == hh8[None, :]) &
                          (jj[:, None] < jj[None, :])).astype(np.float32))
    em0 = np.ones((128, 1), np.float32); em0[0:2] = 0.0
    em4 = np.ones((128, 1), np.float32); em4[2:4] = 0.0
    cst["edge0"], cst["edge4"] = f32(em0), f32(em4)
    masks = []
    for c in range(NCORE):
        od = float(c % 2 == 1)
        masks.append((f32(np.full((128, 1), od)),
                      f32(np.full((128, 1), 1.0 - od))))
    return per_core_x, cst, rope_c, rope_s, masks


class Em:
    ENGINES = ("sync", "tensor", "vector", "act", "gp")

    def __init__(self, nc):
        self.nc = nc
        self.streams = {e: [] for e in self.ENGINES}
        self.sems = {}
        self.sem_val = {}
        self.waited = {e: {} for e in self.ENGINES}

    def sem(self, name):
        if name not in self.sems:
            self.sems[name] = self.nc.alloc_semaphore(name=name)
            self.sem_val[name] = 0
        return self.sems[name]

    def op(self, eng, fn, deps=(), sig=None, inc=1):
        tok = None
        if sig is not None:
            self.sem(sig)
            self.sem_val[sig] += inc
            assert self.sem_val[sig] < 64000, f"sem {sig} overflow"
            tok = (sig, self.sem_val[sig])
        waits = []
        for d in deps:
            if d is None:
                continue
            s, v = d
            w = self.waited[eng]
            if w.get(s, 0) < v:
                w[s] = v
                waits.append((self.sems[s], v))
        sem_h = self.sems[sig] if sig is not None else None

        def run(engobj, waits=waits, fn=fn, sem_h=sem_h, inc=inc):
            for (h, v) in waits:
                engobj.wait_ge(h, v)
            ins = fn(engobj)
            if sem_h is not None:
                ins.then_inc(sem_h, inc)
        self.streams[eng].append(run)
        return tok

    def emit(self, block):
        names = {"sync": "sync", "tensor": "tensor", "vector": "vector",
                 "act": "scalar", "gp": "gpsimd"}
        for eng in self.ENGINES:
            ops = self.streams[eng]
            if not ops:
                continue

            def body(engobj, ops=ops):
                for f in ops:
                    f(engobj)
            getattr(block, names[eng])(body)


def _tok(t):
    return isinstance(t, tuple) and len(t) == 2 and isinstance(t[0], str)


def _mx(*toks):
    out = []
    for t in toks:
        if t is None:
            continue
        if _tok(t):
            out.append(t)
        elif isinstance(t, (list, tuple)):
            out.extend(_mx(*t))
        else:
            raise TypeError(f"bad dep {t!r}")
    return out


class Builder:
    def __init__(self, nc, max_phase=MAX_PHASE, groups=None):
        self.groups = groups or [[0, 1], [2, 3], [4, 5], [6, 7]]
        self.nc = nc
        self.em = Em(nc)
        self.max_phase = max_phase
        self.fr_pb = [None] * 8
        self.fr = {}
        self.tok_stream = []
        self._mk_dram()

    def F(self, key):
        return self.fr.get(key)

    def S(self, key, tok):
        self.fr[key] = tok

    def _mk_dram(self):
        nc = self.nc
        self.xh = nc.dram_tensor("x", [D, W], F32, kind="ExternalInput")
        self.outh = nc.dram_tensor("out", [D, TOK], F32, kind="ExternalOutput")
        self.dh = {}
        for n, shp in [("wqkvT", [D, 3 * D]), ("wprojT", [D, D]),
                       ("w1T", [D, DH]), ("w2T", [DH, D]),
                       ("convwb", [128, 128]), ("wqT", [D, D]),
                       ("wkT", [D, D]), ("wvT", [D, D]), ("woT", [D, D]),
                       ("wbaT", [D, 16]), ("bba", [1, 16]),
                       ("bprojF", [128, 8]), ("boF", [128, 8]),
                       ("ropec", [W, 512]), ("ropes", [W, 512]),
                       ("identity", [128, 128]), ("ones_col", [128, 1]),
                       ("ones_row", [1, 512]), ("maskT", [128, 128]),
                       ("edge0", [128, 1]), ("edge4", [128, 1]),
                       ("epsln", [128, 1]), ("gamtriT", [128, 128]),
                       ("m_odd", [128, 1]), ("m_even", [128, 1])]:
            self.dh[n] = nc.dram_tensor(n, shp, F32, kind="ExternalInput")
        self.t1h = nc.dram_tensor("t1q", [32 * 128, 8 * 128], BF16, kind="Internal")
        self.t2h = nc.dram_tensor("t2q", [DH, D], BF16, kind="Internal")
        self.qstash = nc.dram_tensor("qstash", [D, W], F32, kind="Internal")
        self.xnstash = nc.dram_tensor("xnstash", [D, W], F32, kind="Internal")
        self.qsb = nc.dram_tensor("qsb", [W, D], F32, kind="Internal")
        self.ksb = nc.dram_tensor("ksb", [W, D], F32, kind="Internal")
        self.vsb = nc.dram_tensor("vsb", [W, D], F32, kind="Internal")
        self.qTst = nc.dram_tensor("qTst", [NCH * 128, HB * 128], F32,
                                   kind="Internal")
        self.ost = nc.dram_tensor("ost", [NCH * 128, HB * 128], F32,
                                  kind="Internal")
        self.ofmst = nc.dram_tensor("ofmst", [D, W], F32, kind="Internal")
        self.oTM = nc.dram_tensor("oTM", [W, D], F32, kind="Internal")
        self.cc_a_in = nc.dram_tensor("cc_a_in", [128, 520], F32, kind="Internal")
        self.cc_a_out = nc.dram_tensor("cc_a_out", [128, 520], F32, kind="Internal")
        self.cc_b_in = nc.dram_tensor("cc_b_in", [128, 1024], F32, kind="Internal")
        self.cc_b_out = nc.dram_tensor("cc_b_out", [128, 1024], F32, kind="Internal")
        self.dbg = {n: nc.dram_tensor("dbg_" + n, [D, W], F32,
                                      kind="ExternalOutput")
                    for n in DEBUG_TAPS}
        if "P3C" in DEBUG_TAPS2:
            self.dbgk = nc.dram_tensor("dbgk", [128, 128], F32,
                                       kind="ExternalOutput")
            self.dbgo = nc.dram_tensor("dbgo", [128, 128], F32,
                                       kind="ExternalOutput")

    def alloc_sbuf(self, es):
        nc = self.nc
        sb = lambda n, shp, dt=F32: es.enter_context(nc.sbuf_tensor(n, shp, dt))
        self.xres = sb("xres", [128, 8 * W])
        self.xn = sb("xn", [128, 8 * 512])
        self.wbufA = sb("wbufA", [128, 8 * 512])
        self.scr = sb("scr", [128, 4 * 512])
        self.scr2 = sb("scr2", [128, 2 * 512])
        self.bc1 = sb("bc1", [128, 512])
        self.bc2 = sb("bc2", [128, 512])
        self.n1t = sb("n1t", [128, 8 * 512], BF16)
        self.n2c = sb("n2c", [128, 512], BF16)
        self.w2row = sb("w2row", [128, 1024], BF16)
        self.t1b = sb("t1b", [128, 3072], BF16)
        self.hsb = sb("hsb", [128, 1024])
        self.rmaxb = sb("rmaxb", [128, 512])
        self.consts = sb("constsb", [128, 268])
        self.browbuf = sb("browbuf", [1, 16])
        self.onesr = sb("onesr", [1, 128])
        self.cwb = sb("cwb", [128, 128])
        self.gmt = sb("gmt", [128, 128])
        self.bF = sb("bF", [128, 16])
        self.statr = sb("statr", [1, 1536])
        self.dcol = sb("dcol", [128, 8])
        self.arena = sb("arena", [128, 32 * 512])
        self.pb = [es.enter_context(nc.psum_tensor(f"pb{i}", [128, 512], F32))
                   for i in range(8)]
        c = self.consts
        self.IDEN = c[:, 0:128]
        self.MASKT = c[:, 128:256]
        self.ONESC = c[:, 256:257]
        self.MODD = c[:, 257:258]
        self.MEVEN = c[:, 258:259]
        self.S1C = c[:, 259:260]
        self.S2C = c[:, 260:261]
        self.R1C = c[:, 261:262]
        self.R2C = c[:, 262:263]
        self.EDGE0 = c[:, 263:264]
        self.EDGE4 = c[:, 264:265]
        self.EPSC = c[:, 265:266]

    DMA_RING = 32

    def dma(self, dst, src, deps=(), sig=None, slow=False, lane=None):
        if lane == "w":
            i = getattr(self, "_dma_w", 0)
            self._dma_w = i + 1
            ring, nring, pref, q = i % 8, 8, "w", "gp"
        elif lane == "y":
            i = getattr(self, "_dma_y", 0)
            self._dma_y = i + 1
            ring, nring, pref, q = i % 4, 4, "y", "act"
        else:
            i = getattr(self, "_dma_i", 0)
            self._dma_i = i + 1
            ring, nring = i % self.DMA_RING, self.DMA_RING
            pref, q = "d", "sync"
        prev = (f"{pref}{ring}", 16 * (i // nring)) if i >= nring else None
        return self.em.op(q,
                          lambda e, dd=dst, ss=src, sl=slow: e.dma_start(
                              dd, ss, allow_slow_non_contiguous=sl),
                          deps=_mx(list(deps), prev), sig=f"{pref}{ring}",
                          inc=16)

    def mm(self, out, lhsT, rhs, start, stop, deps=(), sig="t", sgc=False):
        return self.em.op("tensor",
                          lambda e, o=out, l=lhsT, r=rhs, s1=start, s2=stop,
                          g=sgc: e.matmul(o, l, r, start=s1, stop=s2,
                                          skip_group_check=g), deps=deps,
                          sig=sig)

    def tr(self, out, in_, deps=(), sig="t", par=128):
        return self.em.op("tensor",
                          lambda e, o=out, i=in_, p=par:
                          e.transpose(o, i, self.IDEN[0:p, 0:p]),
                          deps=deps, sig=sig)

    def v(self, fn, deps=(), sig="v"):
        return self.em.op("vector", fn, deps=deps, sig=sig)

    def a(self, fn, deps=(), sig="a"):
        return self.em.op("act", fn, deps=deps, sig=sig)

    def g(self, fn, deps=(), sig="g"):
        return self.em.op("gp", fn, deps=deps, sig=sig)

    def phase0(self):
        dh = self.dh
        self.t_lib = None
        tx = [self.dma(self.xres[:, cc * W:(cc + 1) * W],
                       self.xh[cc * 128:(cc + 1) * 128, :]) for cc in range(8)]
        c = self.consts
        tc = [self.dma(c[:, 0:128], dh["identity"][:, :]),
              self.dma(c[:, 128:256], dh["maskT"][:, :]),
              self.dma(c[:, 256:257], dh["ones_col"][:, :]),
              self.dma(c[:, 257:258], dh["m_odd"][:, :]),
              self.dma(c[:, 258:259], dh["m_even"][:, :]),
              self.dma(c[:, 263:264], dh["edge0"][:, :]),
              self.dma(c[:, 264:265], dh["edge4"][:, :]),
              self.dma(c[:, 265:266], dh["epsln"][:, :]),
              self.dma(self.onesr[:, :], dh["ones_row"][0:1, 0:128]),
              self.dma(self.cwb[:, :], dh["convwb"][:, :]),
              self.dma(self.gmt[:, :], dh["gamtriT"][:, :]),
              self.dma(self.bF[:, 0:8], dh["bprojF"][:, :]),
              self.dma(self.bF[:, 8:16], dh["boF"][:, :]),
              self.dma(self.browbuf[:, :], dh["bba"][:, :])]
        self.tok_const = tc
        self.tok_x = tx
        td1 = self._quant_w(dh["w1T"], 8, DH, self.S1C, self.R1C, True)
        td2 = self._quant_w(dh["w2T"], 32, D, self.S2C, self.R2C, False)
        self.tok_wq = [td1, td2]
        self.tok_stream = tx
        self.S("wA", td2)
        self.S("scr2", td2)
        self.S("n2cA", td2)
        self.S("t1b0", td2)

    def _quant_w(self, srcT, kch, free, s_col, r_col, is_w1):
        ntile = free // 512
        n = kch * ntile
        prev3 = getattr(self, "_qw_last_t3", [None] * 4)
        taccs = []
        tred_prev = [None] * 8
        tacc = None
        for i in range(n):
            kc, ft = i // ntile, i % ntile
            sl = i % 8
            buf = self.wbufA[:, sl * 512:sl * 512 + 512]
            tld = self.dma(buf, srcT[kc * 128:(kc + 1) * 128,
                                     ft * 512:(ft + 1) * 512],
                           deps=_mx(tred_prev[sl], prev3[sl % 4]
                                    if i < 8 and sl < 4 else None), lane="w")
            tred = self.v(lambda e, bb=buf, o=self.scr2[:, 4 + sl:5 + sl]:
                          e.reduce_sum(o, bb, axis=AX.X,
                                       apply_absolute_value=True),
                          deps=_mx(tld, taccs[i - 8] if i >= 8 else None))
            tred_prev[sl] = tred
            if i == 0:
                tacc = self.v(lambda e: e.tensor_copy(
                    self.scr2[:, 1:2], self.scr2[:, 4:5]), deps=_mx(tred))
            else:
                tacc = self.v(lambda e, ii=self.scr2[:, 4 + sl:5 + sl]:
                              e.tensor_add(self.scr2[:, 1:2],
                                           self.scr2[:, 1:2], ii),
                              deps=_mx(tred, tacc))
            taccs.append(tacc)
        pb = self.pb
        tpm = self.mm(pb[6][0:1, 0:1], self.ONESC, self.scr2[:, 1:2],
                      True, True,
                      deps=_mx(tacc, self.tok_const, self.fr_pb[6]))
        tpe = self.v(lambda e: e.tensor_copy(self.scr2[0:1, 2:3],
                                             pb[6][0:1, 0:1]), deps=_mx(tpm))
        self.fr_pb[6] = tpe
        tbm = self.mm(pb[7][0:128, 0:1], self.onesr[0:1, 0:128],
                      self.scr2[0:1, 2:3], True, True,
                      deps=_mx(tpe, self.fr_pb[7]))
        tpar = self.v(lambda e: e.tensor_copy(self.scr2[:, 2:3],
                                              pb[7][:, 0:1]), deps=_mx(tbm))
        self.fr_pb[7] = tpar
        tm = self.v(lambda e: e.tensor_scalar(
            self.scr2[:, 3:4], self.scr2[:, 2:3], 1.0 / (kch * 128 * free),
            1e-5, ALU.mult, ALU.max), deps=_mx(tpar))
        self.v(lambda e, o=s_col: e.tensor_copy(o, self.scr2[:, 3:4]),
               deps=_mx(tm))
        trc = self.v(lambda e, o=r_col: e.reciprocal(o, self.scr2[:, 3:4]),
                     deps=_mx(tm))
        qouts = [(self.n2c, 0, 512), (self.t1b, 0, 3072),
                 (self.t1b, 512, 3072), (self.t1b, 1024, 3072)]
        t3_prev = [None] * 4
        tst_prev = getattr(self, "_qw_last_st", [None] * 4)
        tk = None
        for i in range(n):
            kc, ft = i // ntile, i % ntile
            sl = i % 4
            buf = self.wbufA[:, sl * 512:sl * 512 + 512]
            qt, qoff, qstride = qouts[sl]
            qout = qt[:, qoff:qoff + 512]
            tld = self.dma(buf, srcT[kc * 128:(kc + 1) * 128,
                                     ft * 512:(ft + 1) * 512],
                           deps=_mx(t3_prev[sl],
                                    tred_prev[sl], tred_prev[4 + sl]),
                           lane="w")
            t1 = self.a(lambda e, bb=buf: e.activation(
                bb, bb, AF.Copy, scale=r_col), deps=_mx(tld, trc))
            t2 = self.v(lambda e, bb=buf: e.tensor_scalar(
                bb, bb, MAGIC, -MAGIC, ALU.add, ALU.add), deps=_mx(t1))
            t3 = self.v(lambda e, bb=buf, q=qout: e.tensor_scalar(
                q, bb, 1.0, -1.0, ALU.min, ALU.max),
                deps=_mx(t2, tst_prev[sl]))
            t3_prev[sl] = t3
            if is_w1:
                src = bass.AP(qt, qoff, [[qstride, 128], [128, 4], [1, 128]])
                dst = bass.AP(self.t1h,
                              ft * 4 * 128 * 1024 + kc * 128,
                              [[1024, 128], [128 * 1024, 4], [1, 128]])
                tk = [self.dma(dst, src, deps=_mx(t3), lane="y")]
            else:
                tk = [self.dma(self.t2h[kc * 128:(kc + 1) * 128,
                                        ft * 512:(ft + 1) * 512],
                               qout, deps=_mx(t3), lane="y")]
            tst_prev[sl] = tk
        self._qw_last_t3 = t3_prev
        self._qw_last_st = list(tst_prev)
        return _mx(*tst_prev)

    def emit_ln(self, t0, tw, deps=(), want_absmax=False, use_stream=True):
        p_sum, p_sq = self.pb[4][0:1, 0:tw], self.pb[5][0:1, 0:tw]
        xres, xn, scr = self.xres, self.xn, self.scr
        strm = self.tok_stream if use_stream else None
        tmm = tmm2 = None
        for cc in range(8):
            xr = xres[:, cc * W + t0: cc * W + t0 + tw]
            tmm = self.mm(p_sum, self.ONESC, xr, cc == 0, cc == 7,
                          deps=_mx(deps, self.tok_const, self.fr_pb[4],
                                   strm))
        tsqd = [self.F("scr"), self.F("scr")]
        for cc in range(8):
            xr = xres[:, cc * W + t0: cc * W + t0 + tw]
            sq = scr[:, (cc % 2) * 512:(cc % 2) * 512 + tw]
            tsq = self.a(lambda e, o=sq, i=xr: e.activation(o, i, AF.Square),
                         deps=_mx(deps, tsqd[cc % 2], strm))
            tmm2 = self.mm(p_sq, self.ONESC, sq, cc == 0, cc == 7,
                           deps=_mx(tsq, self.fr_pb[5] if cc == 0 else None))
            tsqd[cc % 2] = tmm2
        st = self.statr
        m_r = st[0:1, 0:tw]
        v_r = st[0:1, 512:512 + tw]
        t_r = st[0:1, 512:512 + tw]
        g_r = st[0:1, 1024:1024 + tw]
        b_r = st[0:1, 512:512 + tw]
        t1 = self.v(lambda e: e.tensor_scalar(m_r, p_sum, 1.0 / D, None,
                                              ALU.mult), deps=_mx(tmm))
        t2 = self.v(lambda e: e.tensor_scalar(v_r, p_sq, 1.0 / D, None,
                                              ALU.mult), deps=_mx(tmm2))
        self.fr_pb[4] = t1
        self.fr_pb[5] = t2
        t3 = self.v(lambda e: e.scalar_tensor_tensor(
            g_r, m_r, -1.0, m_r, ALU.mult, ALU.mult), deps=_mx(t1))
        t4 = self.v(lambda e: e.tensor_add(v_r, v_r, g_r), deps=_mx(t2, t3))
        t5 = self.a(lambda e: e.activation(
            t_r, v_r, AF.Sqrt, bias=self.EPSC[0:1, 0:1]),
            deps=_mx(t4, self.tok_const))
        t6 = self.v(lambda e: e.reciprocal(g_r, t_r), deps=_mx(t5, t3))
        t7 = self.v(lambda e: e.scalar_tensor_tensor(
            b_r, m_r, -1.0, g_r, ALU.mult, ALU.mult), deps=_mx(t6))
        t8m = self.mm(self.pb[4][0:128, 0:tw], self.onesr[0:1, 0:128],
                      g_r, True, True, deps=_mx(t6, self.tok_const, t1))
        t8 = self.v(lambda e: e.tensor_copy(self.bc1[:, 0:tw],
                                            self.pb[4][0:128, 0:tw]),
                    deps=_mx(t8m, self.F("bc1")))
        self.fr_pb[4] = t8
        t9m = self.mm(self.pb[5][0:128, 0:tw], self.onesr[0:1, 0:128],
                      b_r, True, True, deps=_mx(t7, self.tok_const, t2))
        t9 = self.v(lambda e: e.tensor_copy(self.bc2[:, 0:tw],
                                            self.pb[5][0:128, 0:tw]),
                    deps=_mx(t9m, self.F("bc2")))
        self.fr_pb[5] = t9
        tn = []
        for cc in range(8):
            xr = xres[:, cc * W + t0: cc * W + t0 + tw]
            xo = xn[:, cc * 512: cc * 512 + tw]
            ta = self.v(lambda e, o=xo, i=xr: e.tensor_mul(
                o, i, self.bc1[:, 0:tw]), deps=_mx(t8, self.F("xn")))
            tb = self.v(lambda e, o=xo: e.tensor_add(o, o, self.bc2[:, 0:tw]),
                        deps=_mx(ta, t9))
            tn.append(tb)
        self.S("bc1", tn)
        self.S("bc2", tn)
        self.S("scr", tmm2)
        tok_amax = None
        if want_absmax:
            trd = self.v(lambda e: e.tensor_reduce(
                self.rmaxb[:, 0:tw],
                bass.AP(self.xn, 0, [[8 * 512, 128], [1, tw], [512, 8]]),
                AX.X, ALU.max, apply_absolute_value=True),
                deps=_mx(tn, self.F("rmaxb")))
            tok_amax = self._pmax_bcast(self.rmaxb, tw, self.scr, 0,
                                        _mx(trd, self.F("scr")))
            self.S("rmaxb", self.pm_readers)
            self.S("scr", self.pm_scr_readers)
            self.S("bc1", None)
        return tn, tok_amax

    def _pmax_bcast(self, src, tw, scr_buf, c0, deps):
        pb = self.pb
        self.pm_readers = []
        self.pm_scr_readers = []
        rows = []
        for ci, cs in enumerate(range(0, tw, 128)):
            cw = min(128, tw - cs)
            ttr = self.tr(pb[4][0:cw, 0:128], src[:, cs:cs + cw],
                          deps=_mx(deps, self.fr_pb[4]))
            self.pm_readers.append(ttr)
            trc = self.v(lambda e, o=scr_buf[0:cw, c0 + ci:c0 + ci + 1],
                         i=pb[4][0:cw, 0:128]:
                         e.tensor_reduce(o, i, AX.X, ALU.max),
                         deps=_mx(ttr, deps))
            self.fr_pb[4] = trc
            ttr2 = self.tr(pb[5][0:1, cs:cs + cw],
                           scr_buf[0:cw, c0 + ci:c0 + ci + 1],
                           deps=_mx(trc, self.fr_pb[5]), par=cw)
            tre = self.v(lambda e,
                         o=scr_buf[0:1, c0 + 512 + cs:c0 + 512 + cs + cw],
                         i=pb[5][0:1, cs:cs + cw]: e.tensor_copy(o, i),
                         deps=_mx(ttr2))
            self.fr_pb[5] = tre
            rows.append(tre)
            self.pm_scr_readers += [ttr2, tre]
        tbm = self.mm(pb[4][0:128, 0:tw], self.onesr[0:1, 0:128],
                      scr_buf[0:1, c0 + 512:c0 + 512 + tw], True, True,
                      deps=_mx(rows, self.tok_const, self.fr_pb[4]))
        tev = self.v(lambda e: e.tensor_copy(self.bc1[:, 0:tw],
                                             pb[4][0:128, 0:tw]),
                     deps=_mx(tbm, self.F("bc1")))
        self.fr_pb[4] = tev
        self.pm_scr_readers.append(tbm)
        return tev

    def phase1(self):
        A, dh, pb = self.arena, self.dh, self.pb
        for k in ("kvt0", "kvt1", "kvt2", "kvt3", "atttm", "attf", "qf"):
            self.S(k, _mx(self.F(k), self.tok_stream))
        kvt = [A[:, 0:2048], A[:, 2048:4096],
               A[:, 10752:12800], A[:, 12800:14848]]
        atttm = A[:, 4096:5120]
        czs = A[:, 5120:5640]
        exA = A[:, 5640:6160]
        qf = A[:, 6656:10752]
        attf = A[:, 10752:14848]
        wq = self.wbufA

        nbt = sum((tw + 127) // 128 for (_, tw) in TILES_W)
        bix = 0
        tq_stash = []
        tCstop = tKstop = None
        
        for ti, (t0, tw) in enumerate(TILES_W):
            tok_xn, _ = self.emit_ln(t0, tw)
            xn_users = []
            for ocg in range(2):
                tld = []
                for oc4 in range(4):
                    oc = ocg * 4 + oc4
                    twl = []
                    for ic in range(8):
                        twl.append(self.dma(
                            wq[:, oc4 * 1024 + ic * 128:
                               oc4 * 1024 + ic * 128 + 128],
                            dh["wqkvT"][ic * 128:(ic + 1) * 128,
                                        oc * 128:(oc + 1) * 128], lane="w",
                            deps=_mx(self.F("wA"))))
                    tld.append(twl)
                self.S("wA", None)
                grp = []
                for oc4 in range(4):
                    oc = ocg * 4 + oc4
                    tmq = None
                    for ic in range(8):
                        lhs = wq[:, oc4 * 1024 + ic * 128:
                                 oc4 * 1024 + ic * 128 + 128]
                        rhs = self.xn[:, ic * 512: ic * 512 + tw]
                        tmq = self.mm(pb[oc4][0:128, 0:tw], lhs, rhs,
                                      ic == 0, ic == 7,
                                      deps=_mx(tok_xn, tld[oc4],
                                               self.fr_pb[oc4]))
                    tev = self.a(lambda e, o=qf[:, oc * 512: oc * 512 + tw],
                                 i=pb[oc4][0:128, 0:tw]:
                                 e.activation(o, i, AF.Relu),
                                 deps=_mx(tmq, self.F("qf")))
                    self.fr_pb[oc4] = tev
                    grp.append(tev)
                    xn_users.append(tmq)
                self.S("wA", grp)
            tq_t = []
            for cc in range(8):
                tq_t.append(self.dma(
                    self.qstash[cc * 128:(cc + 1) * 128, t0:t0 + tw],
                    qf[:, cc * 512: cc * 512 + tw],
                    deps=_mx(grp)))
            tq_stash += tq_t
            self.S("qf", tq_t)
            nblk = (tw + 127) // 128
            kv_ev = {}
            for ot in range(4):
                twl = []
                for ic in range(8):
                    twl.append(self.dma(wq[:, ic * 512: ic * 512 + 512],
                                        dh["wqkvT"][ic * 128:(ic + 1) * 128,
                                                    1024 + ot * 512:
                                                    1024 + ot * 512 + 512],
                                        deps=_mx(self.F("wA")), lane="w"))
                self.S("wA", None)
                lastmm = None
                for bi in range(nblk):
                    bw = min(128, tw - bi * 128)
                    pk = 4 if ot % 2 == 0 else 5
                    tmk = None
                    for ic in range(8):
                        lhs = self.xn[:, ic * 512 + bi * 128:
                                      ic * 512 + bi * 128 + bw]
                        tmk = self.mm(pb[pk][0:bw, 0:512], lhs,
                                      wq[:, ic * 512: ic * 512 + 512],
                                      ic == 0, ic == 7,
                                      deps=_mx(tok_xn, twl, self.fr_pb[pk]))
                    xn_users.append(tmk)
                    lastmm = tmk
                    dst = kvt[bi % 4][0:bw, ot * 512: ot * 512 + 512]
                    if ot < 2:
                        tev = self.a(lambda e, o=dst, i=pb[pk][0:bw, 0:512]:
                                     e.activation(o, i, AF.Relu),
                                     deps=_mx(tmk, self.F(f"kvt{bi % 4}")))
                    else:
                        tev = self.v(lambda e, o=dst, i=pb[pk][0:bw, 0:512]:
                                     e.tensor_copy(o, i),
                                     deps=_mx(tmk, self.F(f"kvt{bi % 4}")))
                    self.fr_pb[pk] = tev
                    kv_ev.setdefault(bi, []).append(tev)
                self.S("wA", lastmm)
            for bi in range(nblk):
                bw = min(128, tw - bi * 128)
                deps_b = _mx(*kv_ev[bi])
                if ti == 0 and bi == 0:
                    tz = self.v(lambda e, bb=kvt[0]: e.tensor_scalar(
                        bb[:, 0:2048], bb[:, 0:2048], self.EDGE0, None,
                        ALU.mult), deps=deps_b)
                    deps_b = _mx(tz)
                if ti == 4:
                    tz = self.v(lambda e, bb=kvt[bi % 4], ww=bw:
                                e.tensor_scalar(bb[0:ww, 0:2048],
                                                bb[0:ww, 0:2048],
                                                self.EDGE4[0:ww, 0:1],
                                                None, ALU.mult), deps=deps_b)
                    deps_b = _mx(tz)
                first = (bix == 0)
                last = (bix == nbt - 1)
                tC = tK = None
                for h in range(HA):
                    hp, hr = h // 2, (h % 2) * 64
                    kh = kvt[bi % 4][0:bw, h * 64: h * 64 + 64]
                    vh = kvt[bi % 4][0:bw, 1024 + h * 64: 1024 + h * 64 + 64]
                    st_ = first and h < 2
                    sp_ = last and h >= HA - 2
                    tC = self.mm(pb[6][hr:hr + 64, hp * 64: hp * 64 + 64],
                                 kh, vh, st_, sp_, deps=deps_b, sgc=True)
                    tK = self.mm(pb[7][hr:hr + 64, hp:hp + 1], kh,
                                 self.ONESC[0:bw, 0:1], st_, sp_,
                                 deps=deps_b, sgc=True)
                self.S(f"kvt{bi % 4}", [tC, tK])
                bix += 1
                if last:
                    tCstop, tKstop = tC, tK
            self.S("xn", xn_users)
        self.S("attf", [tCstop, tKstop])
        te1 = self.v(lambda e: e.tensor_copy(exA[:, 0:512], pb[6][:, 0:512]),
                     deps=_mx(tCstop))
        te2 = self.v(lambda e: e.tensor_copy(exA[:, 512:520], pb[7][:, 0:8]),
                     deps=_mx(tKstop))
        self.fr_pb[6] = te1
        self.fr_pb[7] = te2
        td1 = self.dma(self.cc_a_in[:, :], exA[:, :], deps=_mx(te1, te2))
        tcc = self.g(lambda e: e.collective_compute(
            "AllReduce", ALU.add,
            replica_groups=self.groups,
            ins=[self.cc_a_in.ap().opt()], outs=[self.cc_a_out.ap().opt()]),
            deps=_mx(td1, self.t_lib), sig="cc")
        tcz = self.dma(czs[:, :], self.cc_a_out[:, :], deps=_mx(tcc))
        czbd = A[:, 14848:15888]
        tb0 = self.v(lambda e: e.memset(czbd[:, :], 0.0),
                     deps=_mx(self.F("czbd")))
        tbds = []
        for p in range(8):
            tbds.append(self.v(
                lambda e, o=czbd[0:64, p * 128: p * 128 + 64],
                i=czs[0:64, p * 64:(p + 1) * 64]: e.tensor_copy(o, i),
                deps=_mx(tb0, tcz)))
            tbds.append(self.v(
                lambda e, o=czbd[64:128, p * 128 + 64: p * 128 + 128],
                i=czs[64:128, p * 64:(p + 1) * 64]: e.tensor_copy(o, i),
                deps=_mx(tb0, tcz)))
            tbds.append(self.v(
                lambda e, o=czbd[0:64, 1024 + 2 * p: 1024 + 2 * p + 1],
                i=czs[0:64, 512 + p: 513 + p]: e.tensor_copy(o, i),
                deps=_mx(tb0, tcz)))
            tbds.append(self.v(
                lambda e, o=czbd[64:128, 1025 + 2 * p: 1026 + 2 * p],
                i=czs[64:128, 512 + p: 513 + p]: e.tensor_copy(o, i),
                deps=_mx(tb0, tcz)))
        tcz = tbds
        res_all = []
        for ti, (t0, tw) in enumerate(TILES_W):
            tqb = []
            for cc in range(8):
                tqb.append(self.dma(
                    qf[:, cc * 512: cc * 512 + tw],
                    self.qstash[cc * 128:(cc + 1) * 128, t0:t0 + tw],
                    deps=_mx(tq_stash if ti == 0 else None, self.F("qf")),
                    lane="w"))
            nblk = (tw + 127) // 128
            attf_toks = []
            qf_readers = []
            for bi in range(nblk):
                bw = min(128, tw - bi * 128)
                pkz = bi % 2
                tm2s = []
                for p in range(8):
                    qp = qf[0:128, p * 512 + bi * 128:
                            p * 512 + bi * 128 + bw]
                    tm2 = self.mm(pb[pkz][0:bw, 2 * p:2 * p + 2], qp,
                                  czbd[:, 1024 + 2 * p: 1026 + 2 * p],
                                  True, True,
                                  deps=_mx(tcz, tqb, self.fr_pb[pkz]
                                           if p == 0 else None))
                    tm2s.append(tm2)
                    qf_readers.append(tm2)
                tz1 = self.v(lambda e, ww=bw, i=pb[pkz][0:bw, 0:16]:
                             e.tensor_scalar(
                                 self.scr2[0:ww, 0:16], i,
                                 EPS_A, None, ALU.add),
                             deps=_mx(tm2s, self.F("scr2")))
                tz2 = self.v(lambda e, ww=bw: e.reciprocal(
                    self.scr2[0:ww, 16:32], self.scr2[0:ww, 0:16]),
                    deps=_mx(tz1))
                self.fr_pb[pkz] = tz1
                tz_all = []
                for h in range(HA):
                    hp, hr = h // 2, (h % 2) * 64
                    qh = qf[hr:hr + 64, hp * 512 + bi * 128:
                            hp * 512 + bi * 128 + bw]
                    bk = 2 + h % 2
                    tm1 = self.mm(pb[bk][0:bw, 0:64], qh,
                                  czs[hr:hr + 64, hp * 64: hp * 64 + 64],
                                  True, True,
                                  deps=_mx(tcz, tqb, self.fr_pb[bk]))
                    qf_readers.append(tm1)
                    tz3 = self.v(lambda e, i=pb[bk][0:bw, 0:64],
                                 ww=bw, hh=h,
                                 o=atttm[0:bw, h * 64: h * 64 + 64]:
                                 e.tensor_scalar(o, i,
                                                 self.scr2[0:ww,
                                                           16 + hh:17 + hh],
                                                 None, ALU.mult),
                                 deps=_mx(tm1, tz2, self.F("atttm")))
                    self.fr_pb[bk] = tz3
                    tz_all.append(tz3)
                self.fr_pb[2] = tz_all
                self.fr_pb[3] = tz_all
                self.S("scr2", tz_all)
                ttr_all = []
                for fc in range(8):
                    pk = 4 + (fc % 2)
                    ttr = self.tr(pb[pk][0:128, 0:bw],
                                  atttm[0:bw, fc * 128:(fc + 1) * 128],
                                  deps=_mx(tz_all, self.fr_pb[pk]), par=bw)
                    tev = self.v(lambda e,
                                 o=attf[:, fc * 512 + bi * 128:
                                        fc * 512 + bi * 128 + bw],
                                 i=pb[pk][0:128, 0:bw]: e.tensor_copy(o, i),
                                 deps=_mx(ttr, self.F("attf")))
                    self.fr_pb[pk] = tev
                    ttr_all.append(tev)
                    attf_toks.append(tev)
                self.S("atttm", ttr_all)
            self.S("attf", None)
            res_tile = []
            for oc in range(8):
                pk = 4 + (oc % 2)
                wo_ = 128 * (oc % 2)
                twl = []
                for fc in range(8):
                    twl.append(self.dma(
                        wq[:, fc * 512 + wo_: fc * 512 + wo_ + 128],
                        dh["wprojT"][fc * 128:(fc + 1) * 128,
                                     oc * 128:(oc + 1) * 128],
                        deps=_mx(self.F(f"wA{oc % 2}")), lane="w"))
                tmp = None
                for fc in range(8):
                    tmp = self.mm(pb[pk][0:128, 0:tw],
                                  wq[:, fc * 512 + wo_: fc * 512 + wo_ + 128],
                                  attf[:, fc * 512: fc * 512 + tw],
                                  fc == 0, fc == 7,
                                  deps=_mx(attf_toks, twl, self.fr_pb[pk]))
                self.S(f"wA{oc % 2}", tmp)
                xslice = self.xres[:, oc * W + t0: oc * W + t0 + tw]
                tres = self.v(lambda e, o=xslice, i=pb[pk][0:128, 0:tw],
                              bc=self.bF[:, oc:oc + 1]:
                              e.scalar_tensor_tensor(o, i, bc, o,
                                                     ALU.add, ALU.add),
                              deps=_mx(tmp, self.tok_const))
                self.fr_pb[pk] = tres
                res_tile.append(tres)
            res_all += res_tile
            self.S("attf", res_tile)
            self.S("qf", qf_readers)
            self.S("wA", _mx(self.F("wA0"), self.F("wA1")))
        self.tok_stream = res_all
        if "p1" in self.dbg:
            self._dump_dbg("p1")

    def _dump_dbg(self, name):
        toks = []
        for cc in range(8):
            toks.append(self.dma(self.dbg[name][cc * 128:(cc + 1) * 128, :],
                                 self.xres[:, cc * W:(cc + 1) * W],
                                 deps=_mx(self.tok_stream), sig="dbgd"))
        self.tok_stream = _mx(self.tok_stream, toks)

    def emit_ffn(self, tag):
        A, pb = self.arena, self.pb
        hpall = A[:, 0:16384]
        self.S("hp", _mx(self.F("hp"), self.tok_stream))
        entry = self.tok_stream
        res_all = []
        NT = len(TILES_F)

        def head(ti):
            f0, fw = TILES_F[ti]
            hw0, hww = f0 - 1, fw + 2
            tok_xn, tok_amax = self.emit_ln(hw0, hww, deps=entry,
                                            want_absmax=True,
                                            use_stream=False)
            tca = self.v(lambda e, w=hww: e.tensor_scalar(
                self.bc1[:, 0:w], self.bc1[:, 0:w], 1e-5, 1.0 / 127.0,
                ALU.max, ALU.mult), deps=_mx(tok_amax))
            tcb = self.v(lambda e, w=hww: e.reciprocal(self.bc2[:, 0:w],
                                                       self.bc1[:, 0:w]),
                         deps=_mx(tca, self.F("bc2")))
            tcs = self.v(lambda e, w=hww: e.tensor_scalar(
                self.scr2[:, 512:512 + w], self.bc1[:, 0:w], self.S1C,
                None, ALU.mult),
                deps=_mx(tca, self.F("hsc"), self.F("scr2"), self.tok_wq))
            tn1 = []
            for cc in range(8):
                tu = self.v(lambda e, w=hww,
                            i=self.xn[:, cc * 512: cc * 512 + hww],
                            o=self.scr[:, 0:hww]:
                            e.tensor_mul(o, i, self.bc2[:, 0:w]),
                            deps=_mx(tok_xn, tcb, self.F("scr")))
                tq = self.v(lambda e, i=self.scr[:, 0:hww],
                            o=self.n1t[:, cc * 512: cc * 512 + hww]:
                            e.tensor_scalar(o, i, MAGIC, -MAGIC, ALU.add,
                                            ALU.add),
                            deps=_mx(tu, self.F("n1t")))
                self.S("scr", tq)
                tn1.append(tq)
            self.S("xn", tn1)
            self.S("bc2", tn1)
            self.S("bc1", [tca, tcb, tcs])
            return dict(f0=f0, fw=fw, hww=hww, tok_xn=tok_xn,
                        tca=tca, tcb=tcb, tcs=tcs, tn1=tn1)

        def mm1(ti, s):
            f0, fw, hww = s["f0"], s["fw"], s["hww"]
            tn1, tcs = s["tn1"], s["tcs"]
            silus, mm1t, tevs = [], [], []
            for cc in range(32):
                wsl, pk, hsl = cc % 3, cc % 4, cc % 2
                t1s = self.t1b[:, wsl * 1024:(wsl + 1) * 1024]
                tl = self.dma(t1s, self.t1h[cc * 128:(cc + 1) * 128, :],
                              deps=_mx(self.F(f"t1b{wsl}"),
                                       self.tok_wq if cc < 3 else None),
                              lane="w")
                tmh = None
                for ic in range(8):
                    tmh = self.mm(pb[pk][0:128, 0:hww],
                                  t1s[:, ic * 128:(ic + 1) * 128],
                                  self.n1t[:, ic * 512: ic * 512 + hww],
                                  ic == 0, ic == 7,
                                  deps=_mx(tn1, tl, self.fr_pb[pk]))
                self.S(f"t1b{wsl}", tmh)
                mm1t.append(tmh)
                hss = self.hsb[:, hsl * 512: hsl * 512 + hww]
                tev = self.v(lambda e, w=hww, o=hss, i=pb[pk][0:128, 0:hww]:
                             e.tensor_mul(o, i, self.scr2[:, 512:512 + w]),
                             deps=_mx(tmh, tcs, self.F(f"hsb{hsl}")))
                self.fr_pb[pk] = tev
                tevs.append(tev)
                sca = self.scr[:, (2 + hsl) * 512: (2 + hsl) * 512 + fw]
                ta1 = self.a(lambda e, o=sca, i=hss[:, 0:fw],
                             sc1=self.cwb[:, cc:cc + 1]:
                             e.activation(o, i, AF.Copy, scale=sc1),
                             deps=_mx(tev, self.F(f"cscr{hsl}")))
                ta2 = self.v(lambda e, o=sca, i=hss[:, 2:2 + fw],
                             sc1=self.cwb[:, 64 + cc:65 + cc]:
                             e.scalar_tensor_tensor(o, i, sc1, o,
                                                    ALU.mult, ALU.add),
                             deps=_mx(ta1))
                ta3 = self.v(lambda e, o=sca, i=hss[:, 1:1 + fw],
                             sc1=self.cwb[:, 32 + cc:33 + cc]:
                             e.scalar_tensor_tensor(o, i, sc1, o,
                                                    ALU.mult, ALU.add),
                             deps=_mx(ta2))
                self.S(f"hsb{hsl}", ta3)
                hpd = hpall[:, cc * 512: cc * 512 + fw]
                ts1 = self.a(lambda e, o=hpd, i=sca,
                             bb=self.cwb[:, 96 + cc:97 + cc]:
                             e.activation(o, i, AF.Silu, bias=bb),
                             deps=_mx(ta3, self.F("hp")))
                self.S(f"cscr{hsl}", ts1)
                silus.append(ts1)
            self.S("n1t", mm1t)
            self.S("hsc", tevs)
            trd2 = self.v(lambda e, w=fw: e.tensor_reduce(
                self.hsb[:, 512:512 + w],
                bass.AP(self.arena, 0, [[32 * 512, 128], [1, w], [512, 32]]),
                AX.X, ALU.max, apply_absolute_value=True),
                deps=_mx(silus, self.F("hsb1")))
            s.update(trd2=trd2)

        def tail(ti, s):
            fw = s["fw"]
            tp2 = self._pmax_bcast(self.hsb[:, 512:1024], fw, self.xn, 0,
                                   _mx(s["trd2"], self.F("xn")))
            self.S("hsb1", self.pm_readers + [s["trd2"]])
            self.S("xn", self.pm_scr_readers)
            tc2a = self.v(lambda e, w=fw: e.tensor_scalar(
                self.bc1[:, 0:w], self.bc1[:, 0:w], 1e-5, 1.0 / 127.0,
                ALU.max, ALU.mult), deps=_mx(tp2))
            tc2b = self.v(lambda e, w=fw: e.reciprocal(self.hsb[:, 0:w],
                                                       self.bc1[:, 0:w]),
                          deps=_mx(tc2a, self.F("hsb0")))
            tc2s = self.v(lambda e, w=fw: e.tensor_scalar(
                self.scr2[:, 0:w], self.bc1[:, 0:w], self.S2C, None,
                ALU.mult), deps=_mx(tc2a, self.F("s2rd"), self.F("scr2")))
            self.S("bc1", [tc2b, tc2s])
            s.update(tc2b=tc2b, tc2s=tc2s)

        def mm2(ti, s, s_next):
            f0, fw = s["f0"], s["fw"]
            tc2b, tc2s = s["tc2b"], s["tc2s"]
            wmms = [None] * 8
            tn2r = []
            for cc in range(32):
                wsl = cc % 3
                wbuf = self.t1b[:, wsl * 1024:(wsl + 1) * 1024]
                wkey = f"t1b{wsl}"
                nsl = cc % 2
                nbuf = (self.n2c[:, 0:fw] if nsl == 0
                        else self.w2row[:, 0:fw])
                nkey = "n2cA" if nsl == 0 else "n2cB"
                tu = self.v(lambda e, w=fw,
                            i=hpall[:, cc * 512: cc * 512 + fw],
                            o=self.scr[:, 0:fw]:
                            e.tensor_mul(o, i, self.hsb[:, 0:w]),
                            deps=_mx(tc2b, self.F("scr")))
                tn2r.append(tu)
                tq = self.v(lambda e, i=self.scr[:, 0:fw], o=nbuf:
                            e.tensor_scalar(o, i, MAGIC, -MAGIC, ALU.add,
                                            ALU.add),
                            deps=_mx(tu, self.F(nkey)))
                self.S("scr", tq)
                tw2 = self.dma(wbuf, self.t2h[cc * 128:(cc + 1) * 128, :],
                               deps=_mx(self.F(wkey)), lane="w")
                mmcc = []
                for oc in range(8):
                    tmw = self.mm(pb[oc][0:128, 0:fw],
                                  wbuf[:, oc * 128:(oc + 1) * 128],
                                  nbuf, cc == 0, cc == 31,
                                  deps=_mx(tq, tw2,
                                           self.fr_pb[oc] if cc == 0
                                           else None))
                    mmcc.append(tmw)
                    wmms[oc] = tmw
                self.S(nkey, mmcc)
                self.S(wkey, mmcc)
            res_tile = []
            tscs = []
            last = (ti == NT - 1)
            for oc in range(8):
                sb2 = self.scr[:, (2 + oc % 2) * 512:(2 + oc % 2) * 512 + fw]
                tsc = self.v(lambda e, w=fw, i=pb[oc][0:128, 0:fw], o=sb2:
                             e.tensor_mul(o, i, self.scr2[:, 0:w]),
                             deps=_mx(wmms[oc], tc2s,
                                      self.F("cscr%d" % (oc % 2))))
                tscs.append(tsc)
                wnow = fw if last else fw - 1
                xslice = self.xres[:, oc * W + f0: oc * W + f0 + wnow]
                tres = self.v(lambda e, o=xslice, i=sb2[:, 0:wnow]:
                              e.tensor_add(o, o, i), deps=_mx(tsc))
                toks_cs = [tres]
                if not last:
                    tcp = self.v(lambda e, o=self.dcol[:, oc:oc + 1],
                                 i=sb2[:, fw - 1:fw]: e.tensor_copy(o, i),
                                 deps=_mx(tsc, self.F("dcol%d" % oc)))
                    tdef = self.v(lambda e,
                                  o=self.xres[:, oc * W + f0 + fw - 1:
                                              oc * W + f0 + fw],
                                  i=self.dcol[:, oc:oc + 1]:
                                  e.tensor_add(o, o, i),
                                  deps=_mx(tcp, s_next["tok_xn"]))
                    self.S("dcol%d" % oc, tdef)
                    res_all.append(tdef)
                    toks_cs += [tcp, tdef]
                self.S("cscr%d" % (oc % 2), toks_cs)
                self.fr_pb[oc] = tres
                res_tile.append(tres)
            res_all.extend(res_tile)
            self.S("hp", tn2r)
            self.S("hsb0", tn2r + [tc2b])
            self.S("s2rd", tscs)
            s["tscs"] = tscs

        st = head(0)
        for ti in range(NT):
            mm1(ti, st)
            st_next = head(ti + 1) if ti + 1 < NT else None
            tail(ti, st)
            mm2(ti, st, st_next)
            st = st_next
        self.S("scr2", _mx(self.F("hsc"), self.F("s2rd")))
        self.tok_stream = _mx(entry, res_all)
        if tag in self.dbg:
            self._dump_dbg(tag)

    def phase3(self):
        A, dh, pb = self.arena, self.dh, self.pb
        for k in ("qtm", "ktm", "vtm", "otmh", "qTh", "kTh", "pmh",
                  "bal", "sS0", "sS1", "p3hs", "p3co"):
            self.S(k, _mx(self.F(k), self.tok_stream))
        tok_arena0 = self.tok_stream
        qtm = A[:, 0:1024]
        ktm = A[:, 1024:2048]
        vtm = A[:, 2048:3072]
        sS = [A[:, 3072:4096], A[:, 4096:5120]]
        sIn = A[:, 5120:6144]
        bal = A[:, 6144:6432]
        bR = A[:, 6432:6560]
        cR = A[:, 6560:6688]
        icR = A[:, 6688:6816]
        cTt = A[:, 6816:6952]
        icTt = A[:, 6952:7088]
        ksc = A[:, 7088:7224]
        grep = A[:, 7224:7352]
        gtrep = A[:, 7352:7480]
        zc = A[:, 7480:7608]
        gam8 = A[:, 7608:7624]
        gexc = A[:, 7624:7640]
        grow = A[:, 7640:7768]
        gtrow = A[:, 7768:7896]
        bhal = A[:, 7896:7912]
        qTh = A[:, 7912:8040]
        kTh = A[:, 8040:8168]
        qT8s = A[:, 9320:10344]
        pmh = A[:, 8168:8296]
        otmh = A[:, 8296:9320]
        hrow = A[:, 9320:10344]
        hcol = A[:, 10344:10352]
        exB = A[:, 10352:11376]
        wq = self.wbufA

        twba = []
        for ic in range(8):
            twba.append(self.dma(wq[:, ic * 512: ic * 512 + 16],
                                 dh["wbaT"][ic * 128:(ic + 1) * 128, :],
                                 deps=_mx(self.F("wA")), lane="w"))
        self.S("wA", None)
        tba_ev = []
        txn_st = []
        for ti, (t0, tw) in enumerate(PH3_TILES + [(0, 2), (2050, 2)]):
            tok_xn, _ = self.emit_ln(t0, tw)
            tst = []
            for cc in range(8):
                tst.append(self.dma(
                    self.xnstash[cc * 128:(cc + 1) * 128, t0:t0 + tw],
                    self.xn[:, cc * 512: cc * 512 + tw], deps=_mx(tok_xn)))
            txn_st += tst
            if ti < 4:
                cols = [(4 * ti + jj, jj * 128, 128) for jj in range(4)]
            elif t0 == 0:
                cols = []
            else:
                cols = [(16, 0, 1)]
            xn_users = []
            for (slot, xoff, bw) in cols:
                tmb = None
                for ic in range(8):
                    lhs = self.xn[:, ic * 512 + xoff: ic * 512 + xoff + bw]
                    tmb = self.mm(pb[6][0:bw, 0:16], lhs,
                                  wq[:, ic * 512: ic * 512 + 16],
                                  ic == 0, False,
                                  deps=_mx(tok_xn, twba, self.fr_pb[6]))
                tmb2 = self.mm(pb[6][0:bw, 0:16], self.onesr[0:1, 0:bw],
                               self.browbuf[0:1, 0:16], False, True,
                               deps=_mx(self.tok_const))
                xn_users.append(tmb)
                tsg = self.a(lambda e, o=bal[0:bw, slot * 16:(slot + 1) * 16],
                             i=pb[6][0:bw, 0:16]:
                             e.activation(o, i, AF.Sigmoid),
                             deps=_mx(tmb, tmb2, self.F("bal")))
                self.fr_pb[6] = tsg
                tb2 = self.v(lambda e,
                             o=bal[0:bw, slot * 16: slot * 16 + 8]:
                             e.tensor_scalar(o, o, BETA_MAX - BETA_MIN,
                                             BETA_MIN, ALU.mult, ALU.add),
                             deps=_mx(tsg))
                tba_ev.append(tb2)
                tba_ev.append(tsg)
            self.S("xn", _mx(xn_users, tst))
        self.S("wA", tba_ev)
        tqkv_st = []
        for wname, dst in [("wqT", self.qsb), ("wkT", self.ksb),
                           ("wvT", self.vsb)]:
            for half in range(2):
                twl = []
                for ic in range(8):
                    twl.append(self.dma(wq[:, ic * 512: ic * 512 + 512],
                                        dh[wname][ic * 128:(ic + 1) * 128,
                                                  half * 512:
                                                  half * 512 + 512],
                                        deps=_mx(self.F("wA")), lane="w"))
                self.S("wA", None)
                lastmm = None
                for ti, (t0, tw) in enumerate(PH3_TILES + [(0, 2), (2050, 2)]):
                    txl = []
                    for cc in range(8):
                        txl.append(self.dma(
                            self.xn[:, cc * 512: cc * 512 + tw],
                            self.xnstash[cc * 128:(cc + 1) * 128, t0:t0 + tw],
                            lane="w",
                            deps=_mx(txn_st if (wname == "wqT" and half == 0)
                                     else None, self.F("xn"))))
                    xn_users = []
                    nblk = (tw + 127) // 128
                    for bi in range(nblk):
                        bw = min(128, tw - bi * 128)
                        pk = bi % 2
                        tmk = None
                        for ic in range(8):
                            lhs = self.xn[:, ic * 512 + bi * 128:
                                          ic * 512 + bi * 128 + bw]
                            tmk = self.mm(pb[pk][0:bw, 0:512], lhs,
                                          wq[:, ic * 512: ic * 512 + 512],
                                          ic == 0, ic == 7,
                                          deps=_mx(txl, twl, self.fr_pb[pk]))
                        xn_users.append(tmk)
                        lastmm = tmk
                        tev = self.v(lambda e,
                                     o=self.scr[:, pk * 512: pk * 512 + 512]
                                     [0:bw, :],
                                     i=pb[pk][0:bw, 0:512]: e.tensor_copy(o, i),
                                     deps=_mx(tmk, self.F(f"p3scr{pk}")))
                        self.fr_pb[pk] = tev
                        tstd = self.dma(
                            dst[t0 + bi * 128: t0 + bi * 128 + bw,
                                half * 512: half * 512 + 512],
                            self.scr[0:bw, pk * 512: pk * 512 + 512],
                            deps=_mx(tev))
                        self.S(f"p3scr{pk}", tstd)
                        tqkv_st.append(tstd)
                    self.S("xn", xn_users)
                self.S("wA", lastmm)
        dep_bal = tba_ev

        def bal_beta_ap():
            return bass.AP(self.arena, 6144, [[32 * 512, 128], [16, 16], [1, 8]])
        tgb = self.v(lambda e: e.tensor_copy(cR[:, :], bal_beta_ap()),
                     deps=_mx(dep_bal, tok_arena0))
        ttr1 = self.em.op("tensor", lambda e: e.transpose(
            pb[7][0:128, 0:128], cR[:, :], self.IDEN),
            deps=_mx(tgb, self.fr_pb[7]), sig="t")
        tev1 = self.v(lambda e: e.tensor_copy(bR[:, :], pb[7][:, 0:128]),
                      deps=_mx(ttr1))
        self.fr_pb[7] = tev1
        tz0 = self.v(lambda e: e.memset(zc[:, :], 0.0), deps=_mx(tok_arena0))
        tsc1 = self.v(lambda e: e.tensor_tensor_scan(
            cR[:, :], bR[:, :], zc[:, :], 1.0, ALU.mult, ALU.add),
            deps=_mx(tev1, tz0))
        tic = self.v(lambda e: e.reciprocal(icR[:, :], cR[:, :]),
                     deps=_mx(tsc1))
        ttr2 = self.tr(pb[7][0:128, 0:128], cR[:, :], deps=_mx(tsc1, tev1))
        tev2 = self.v(lambda e: e.tensor_copy(
            A[:, 6816:6816 + 128], pb[7][:, 0:128]), deps=_mx(ttr2))
        self.fr_pb[7] = tev2
        ttr3 = self.tr(pb[7][0:128, 0:128], icR[:, :], deps=_mx(tic, tev2))
        tev3 = self.v(lambda e: e.tensor_copy(
            A[:, 6952:6952 + 128], pb[7][:, 0:128]), deps=_mx(ttr3))
        self.fr_pb[7] = tev3
        tks = []
        for j in range(NCH):
            tks.append(self.v(lambda e, jj=j: e.tensor_tensor(
                ksc[:, jj * 8: jj * 8 + 8],
                bal[:, jj * 16 + 8: jj * 16 + 16],
                A[:, 6952 + jj * 8: 6952 + jj * 8 + 8], ALU.mult),
                deps=_mx(tev3, dep_bal)))
        ttr4 = self.tr(pb[7][0:1, 0:128], cR[:, 127:128], deps=_mx(tsc1, tev3))
        tev4 = self.v(lambda e: e.tensor_copy(grow[0:1, 0:128],
                                              pb[7][0:1, 0:128]),
                      deps=_mx(ttr4))
        self.fr_pb[7] = tev4
        tmg = self.mm(pb[7][0:128, 0:128], self.onesr[0:1, 0:128],
                      grow[0:1, 0:128], True, True,
                      deps=_mx(tev4, self.tok_const))
        tevg = self.v(lambda e: e.tensor_copy(grep[:, :], pb[7][:, 0:128]),
                      deps=_mx(tmg))
        self.fr_pb[7] = tevg
        tln = self.a(lambda e: e.activation(A[:, 7608:7609], cR[:, 127:128],
                                            AF.Ln), deps=_mx(tsc1, tevg))
        tmge = self.mm(pb[7][0:128, 0:1], self.gmt[:, :], A[:, 7608:7609],
                       True, True, deps=_mx(tln, self.fr_pb[7],
                                            self.tok_const))
        texp = self.a(lambda e: e.activation(A[:, 7609:7610],
                                             pb[7][:, 0:1], AF.Exp),
                      deps=_mx(tmge))
        self.fr_pb[7] = texp
        ttr5 = self.tr(pb[7][0:1, 0:128], A[:, 7609:7610],
                       deps=_mx(texp, self.fr_pb[7]))
        tevg5 = self.v(lambda e: e.tensor_copy(gtrow[0:1, 0:128],
                                               pb[7][0:1, 0:128]),
                       deps=_mx(ttr5))
        self.fr_pb[7] = tevg5
        tmg2 = self.mm(pb[7][0:128, 0:128], self.onesr[0:1, 0:128],
                       gtrow[0:1, 0:128], True, True, deps=_mx(tevg5, tevg))
        tevg2 = self.v(lambda e: e.tensor_copy(gtrep[:, :], pb[7][:, 0:128]),
                       deps=_mx(tmg2))
        self.fr_pb[7] = tevg2
        tqT_st, tO_st = [], []
        tlast_state = None
        tlast_cross = None
        for j in range(NCH):
            j0 = 2 + j * 128
            tql = self.dma(qtm[:, :], self.qsb[j0:j0 + 128, :],
                           deps=_mx(tqkv_st if j == 0 else None,
                                    self.F("qtm")), lane="w")
            tkl = self.dma(ktm[:, :], self.ksb[j0:j0 + 128, :],
                           deps=_mx(self.F("ktm")), lane="w")
            tvl = self.dma(vtm[:, :], self.vsb[j0:j0 + 128, :],
                           deps=_mx(self.F("vtm")), lane="w")
            trc_ = self.dma(self.scr2[:, 0:512], dh["ropec"][j0:j0 + 128, :],
                            deps=_mx(self.F("p3rc"), self.F("scr2")),
                            lane="w")
            trs_ = self.dma(self.scr2[:, 512:1024],
                            dh["ropes"][j0:j0 + 128, :],
                            deps=_mx(self.F("p3rs")), lane="w")
            self.S("scr2", None)
            tq_r = self._rope(0, trc_, trs_, _mx(tql))
            tk_r = self._rope(1024, trc_, trs_, _mx(tkl))
            self.S("p3rc", [tq_r, tk_r])
            self.S("p3rs", [tq_r, tk_r])
            tqs, tks2 = [], []
            for h in range(HB):
                tqs.append(self.v(lambda e, hh=h,
                                  sc1=A[:, 6816 + j * 8 + h:
                                        6816 + j * 8 + h + 1]:
                                  e.tensor_scalar(
                                      qtm[:, hh * 128:(hh + 1) * 128],
                                      qtm[:, hh * 128:(hh + 1) * 128],
                                      sc1, None, ALU.mult),
                                  deps=_mx(tq_r, tev2)))
                tks2.append(self.v(lambda e, hh=h,
                                   sc1=ksc[:, j * 8 + h: j * 8 + h + 1]:
                                   e.tensor_scalar(
                                       ktm[:, hh * 128:(hh + 1) * 128],
                                       ktm[:, hh * 128:(hh + 1) * 128],
                                       sc1, None, ALU.mult),
                                   deps=_mx(tk_r, tks)))
            o_ev = []
            tcross = []
            ttqs, ttks = [], []
            tevqs, qreads = [], []
            for h in range(HB):
                qsl = qT8s[:, h * 128:(h + 1) * 128]
                ttq = self.tr(pb[2][0:128, 0:128],
                              qtm[:, h * 128:(h + 1) * 128],
                              deps=_mx(tqs[h], self.fr_pb[2]))
                tevq = self.v(lambda e, o=qsl: e.tensor_copy(
                    o, pb[2][:, 0:128]),
                    deps=_mx(ttq, self.F("qTh") if h == 0 else None))
                self.fr_pb[2] = tevq
                ttqs.append(ttq)
                tevqs.append(tevq)
                ttk = self.tr(pb[3][0:128, 0:128],
                              ktm[:, h * 128:(h + 1) * 128],
                              deps=_mx(tks2[h], self.fr_pb[3]))
                tevk = self.v(lambda e: e.tensor_copy(kTh[:, :],
                                                      pb[3][:, 0:128]),
                              deps=_mx(ttk, self.F("kTh")))
                self.fr_pb[3] = tevk
                ttks.append(ttk)
                tmp1 = self.mm(pb[2][0:128, 0:128], kTh[:, :], qsl,
                               True, True,
                               deps=_mx(tevq, tevk, self.fr_pb[2]))
                qreads.append(tmp1)
                tmsk = self.v(lambda e: e.tensor_mul(pmh[:, :],
                                                     pb[2][:, 0:128],
                                                     self.MASKT),
                              deps=_mx(tmp1, self.F("pmh")))
                self.fr_pb[2] = tmsk
                tmo = self.mm(pb[6][0:128, 0:128], pmh[:, :],
                              vtm[:, h * 128:(h + 1) * 128], True,
                              j == 0, deps=_mx(tmsk, tvl, self.fr_pb[6]))
                if j > 0:
                    tmo = self.mm(pb[6][0:128, 0:128],
                                  qT8s[:, h * 128:(h + 1) * 128],
                                  sS[j % 2][:, h * 128:(h + 1) * 128],
                                  False, True,
                                  deps=_mx(tevq, tlast_state))
                    tcross.append(tmo)
                    qreads.append(tmo)
                tevo = self.v(lambda e, hh=h: e.tensor_copy(
                    otmh[:, hh * 128:(hh + 1) * 128], pb[6][:, 0:128]),
                    deps=_mx(tmo, self.F("otmh")))
                self.fr_pb[6] = tevo
                o_ev.append(tevo)
                tmd = self.mm(pb[7][0:128, 0:128],
                              ktm[:, h * 128:(h + 1) * 128],
                              vtm[:, h * 128:(h + 1) * 128], True, True,
                              deps=_mx(tks2[h], tvl, self.fr_pb[7]))
                gcol = grep[:, j * 8 + h: j * 8 + h + 1]
                if j == 0:
                    tst = self.v(lambda e, hh=h, gc=gcol: e.tensor_scalar(
                        sS[1][:, hh * 128:(hh + 1) * 128], pb[7][:, 0:128],
                        gc, None, ALU.mult),
                        deps=_mx(tmd, tevg, self.F("sS1")))
                else:
                    t_g = self.v(lambda e, gc=gcol: e.tensor_scalar(
                        A[:, 14848:14976], pb[7][:, 0:128], gc, None,
                        ALU.mult), deps=_mx(tmd, tevg, self.F("p3st")))
                    dst = sS[(j + 1) % 2][:, h * 128:(h + 1) * 128]
                    src = sS[j % 2][:, h * 128:(h + 1) * 128]
                    tst = self.v(lambda e, o=dst, i=src, gc=gcol:
                                 e.scalar_tensor_tensor(
                                     o, i, gc, A[:, 14848:14976],
                                     ALU.mult, ALU.add),
                                 deps=_mx(t_g, tlast_cross,
                                          self.F("sS%d" % ((j + 1) % 2))))
                    self.S("p3st", tst)
                self.fr_pb[7] = tst
                if h == HB - 1:
                    tlast_state = tst
            if tcross:
                tlast_cross = tcross
            self.S("sS%d" % ((j + 1) % 2), None)
            self.S("qtm", ttqs)
            self.S("ktm", ttks + [tmd])
            self.S("vtm", [tmd, tmo])
            tstq = self.dma(self.qTst[j * 128:(j + 1) * 128, :],
                            qT8s[:, :], deps=_mx(tevqs))
            tqT_st.append(tstq)
            self.S("qTh", qreads + [tstq])
            tso = self.dma(self.ost[j * 128:(j + 1) * 128, :], otmh[:, :],
                           deps=_mx(o_ev))
            tO_st.append(tso)
            self.S("otmh", tso)
            self.S("kTh", tmp1)
            self.S("pmh", tmo)
        thr = self._halo_right(sS[NCH % 2], bal, bhal,
                               _mx(tlast_state, tqkv_st))
        tex1 = self.v(lambda e: e.tensor_scalar(
            exB[:, :], sS[NCH % 2][:, :], self.MEVEN, None, ALU.mult),
            deps=_mx(tlast_state, self.tok_const))
        tex2 = self.dma(self.cc_b_in[:, :], exB[:, :], deps=_mx(tex1))
        tcc = self.g(lambda e: e.collective_compute(
            "AllReduce", ALU.add,
            replica_groups=self.groups,
            ins=[self.cc_b_in.ap().opt()], outs=[self.cc_b_out.ap().opt()]),
            deps=_mx(tex2, self.t_lib), sig="cc")
        tex3 = self.dma(sIn[:, :], self.cc_b_out[:, :],
                        deps=_mx(tcc, self.tok_stream))
        tex4 = self.v(lambda e: e.tensor_scalar(
            sIn[:, :], sIn[:, :], self.MODD, None, ALU.mult), deps=_mx(tex3))
        thl = self._halo_left(sIn, _mx(tex4, tqkv_st))
        tofm_st = [thr, thl]
        qT8 = [A[:, 0:1024], A[:, 1024:2048]]
        otm2 = [otmh, A[:, 2048:3072]]
        kTh8 = A[:, 3072:4096]
        s8 = A[:, 4096:5120]
        self.S("s8", _mx(tex1, thr, tO_st))
        tst_prev = _mx(tex1, thr)
        for j in range(NCH):
            j0 = 2 + j * 128
            sl3 = j % 2
            tol = self.dma(otm2[sl3][:, :],
                           self.ost[j * 128:(j + 1) * 128, :],
                           deps=_mx([tO_st, tex1, thr] if j < 2 else None,
                                    self.F(f"otm2{sl3}")), lane="w")
            tq8 = self.dma(qT8[sl3][:, :], self.qTst[j * 128:(j + 1) * 128, :],
                           deps=_mx([tqT_st, tex1, thr] if j < 2 else None,
                                    self.F(f"qT8{sl3}")), lane="w")
            corr, tevos, qcons = [], [], []
            ts8 = []
            for h in range(HB):
                ts8.append(self.v(
                    lambda e, si=sIn[:, h * 128:(h + 1) * 128],
                    gc=gtrep[:, j * 8 + h: j * 8 + h + 1],
                    o=s8[:, h * 128:(h + 1) * 128]:
                    e.tensor_scalar(o, si, gc, None, ALU.mult),
                    deps=_mx(tex4, tevg2, self.F("s8"))))
            self.S("s8", None)
            tadds = []
            for h in range(HB):
                bk = (2, 3, 6, 7)[h % 4]
                tmc = self.mm(pb[bk][0:128, 0:128],
                              qT8[sl3][:, h * 128:(h + 1) * 128],
                              s8[:, h * 128:(h + 1) * 128], True, True,
                              deps=_mx(tq8, ts8[h], self.fr_pb[bk]))
                qcons.append(tmc)
                tadd = self.v(lambda e, hh=h, o=otm2[sl3], b=bk:
                              e.tensor_add(
                                  o[:, hh * 128:(hh + 1) * 128],
                                  o[:, hh * 128:(hh + 1) * 128],
                                  pb[b][:, 0:128]), deps=_mx(tol, tmc))
                self.fr_pb[bk] = tadd
                tadds.append(tadd)
            self.S("s8", qcons)
            tsto = self.dma(self.oTM[j0:j0 + 128, :], otm2[sl3][:, :],
                            deps=_mx(tadds))
            tofm_st.append(tsto)
            self.S(f"otm2{sl3}", tadds + [tsto])
            self.S(f"qT8{sl3}", qcons)
        res_all = []
        first_wo = True
        for (t0, tw) in WO_TILES:
            tofl = []
            nb = (tw + 127) // 128
            for tb in range(nb):
                r = min(128, tw - tb * 128)
                osl = tb % 2
                obuf = A[:, osl * 1024: osl * 1024 + 1024]
                told = self.dma(obuf[0:r, :],
                                self.oTM[t0 + tb * 128: t0 + tb * 128 + r, :],
                                deps=_mx(tofm_st if first_wo else None,
                                         self.F(f"wotm{osl}")), lane="w")
                first_wo = False
                trds = []
                for fc in range(8):
                    bk = 4 + fc % 2
                    ttr = self.tr(pb[bk][0:128, 0:r],
                                  obuf[0:r, fc * 128:(fc + 1) * 128],
                                  deps=_mx(told, self.fr_pb[bk]), par=r)
                    tev = self.v(lambda e, o=self.xn[:, fc * 512 + tb * 128:
                                                     fc * 512 + tb * 128 + r],
                                 i=pb[bk][0:128, 0:r]: e.tensor_copy(o, i),
                                 deps=_mx(ttr, self.F("xn")))
                    self.fr_pb[bk] = tev
                    trds.append(ttr)
                    tofl.append(tev)
                self.S(f"wotm{osl}", trds)
            xn_users = []
            for oc in range(8):
                pk = oc % 2
                wo_ = 128 * (oc % 2) + 256
                twl = []
                for fc in range(8):
                    twl.append(self.dma(
                        wq[:, fc * 512 + wo_: fc * 512 + wo_ + 128],
                        dh["woT"][fc * 128:(fc + 1) * 128,
                                  oc * 128:(oc + 1) * 128],
                        deps=_mx(self.F(f"wB{oc % 2}")), lane="w"))
                tmp = None
                for fc in range(8):
                    tmp = self.mm(pb[pk][0:128, 0:tw],
                                  wq[:, fc * 512 + wo_: fc * 512 + wo_ + 128],
                                  self.xn[:, fc * 512: fc * 512 + tw],
                                  fc == 0, fc == 7,
                                  deps=_mx(tofl, twl, self.fr_pb[pk]))
                self.S(f"wB{oc % 2}", tmp)
                xn_users.append(tmp)
                xslice = self.xres[:, oc * W + t0: oc * W + t0 + tw]
                tres = self.v(lambda e, o=xslice, i=pb[pk][0:128, 0:tw],
                              bc=self.bF[:, 8 + oc: 9 + oc]:
                              e.scalar_tensor_tensor(o, i, bc, o,
                                                     ALU.add, ALU.add),
                              deps=_mx(tmp, self.tok_const))
                self.fr_pb[pk] = tres
                res_all.append(tres)
            self.S("xn", xn_users)
        self.tok_stream = res_all
        if "p3" in self.dbg:
            self._dump_dbg("p3")

    def _rope(self, base, tcs, tsn, deps):
        scr, scr2, Ah = self.scr, self.scr2, self.arena

        def sl(off):
            return bass.AP(Ah, base + off, [[32 * 512, 128], [128, 8], [2, 64]])

        def sc(off):
            return bass.AP(scr, off, [[4 * 512, 128], [64, 8], [1, 64]])
        cosap = bass.AP(scr2, 0, [[2 * 512, 128], [64, 8], [1, 64]])
        sinap = bass.AP(scr2, 512, [[2 * 512, 128], [64, 8], [1, 64]])
        t1 = self.v(lambda e: e.tensor_tensor(sc(0), sl(0), cosap, ALU.mult),
                    deps=_mx(deps, tcs, self.F("scr")))
        t2 = self.v(lambda e: e.tensor_tensor(sc(512), sl(1), sinap, ALU.mult),
                    deps=_mx(deps, tsn))
        t3 = self.v(lambda e: e.tensor_tensor(sc(1024), sl(1), cosap, ALU.mult),
                    deps=_mx(deps, tcs))
        t4 = self.v(lambda e: e.tensor_tensor(sc(1536), sl(0), sinap, ALU.mult),
                    deps=_mx(deps, tsn))
        t5 = self.v(lambda e: e.tensor_tensor(sl(0), sc(0), sc(512),
                                              ALU.subtract),
                    deps=_mx(t1, t2, t4))
        t6 = self.v(lambda e: e.tensor_tensor(sl(1), sc(1024), sc(1536),
                                              ALU.add), deps=_mx(t3, t5))
        self.S("scr", t6)
        return t6

    def _halo_right(self, s16, bal, bhal, deps):
        A, pb = self.arena, self.pb
        QR, KR, VR, OH = 11376, 12400, 13424, 14448
        tmb = self.mm(pb[3][0:128, 0:16], self.onesr[0:1, 0:128],
                      bal[0:1, 256:272], True, True,
                      deps=_mx(deps, self.fr_pb[3]))
        tbh = self.v(lambda e: e.tensor_copy(bhal[:, :], pb[3][:, 0:16]),
                     deps=_mx(tmb))
        self.fr_pb[3] = tbh
        tq = self.dma(A[0:1, QR:QR + 1024], self.qsb[2050:2051, :],
                      deps=deps, lane="w")
        tk = self.dma(A[0:1, KR:KR + 1024], self.ksb[2050:2051, :],
                      deps=deps, lane="w")
        tv = self.dma(A[0:1, VR:VR + 1024], self.vsb[2050:2051, :],
                      deps=deps, lane="w")
        trc_ = self.dma(self.scr2[0:1, 0:512], self.dh["ropec"][2050:2051, :],
                        deps=_mx(self.F("p3rc")))
        trs_ = self.dma(self.scr2[0:1, 512:1024],
                        self.dh["ropes"][2050:2051, :],
                        deps=_mx(self.F("p3rs")))
        tqr = self._rope_row(QR, trc_, trs_, _mx(tq))
        tkr = self._rope_row(KR, trc_, trs_, _mx(tk))
        self.S("p3rc", [tqr, tkr])
        self.S("p3rs", [tqr, tkr])
        tqc = []
        for h in range(HB):
            ttc = self.em.op("tensor", lambda e, hh=h: e.transpose(
                pb[3][0:128, hh:hh + 1],
                bass.AP(self.arena, QR + hh * 128, [[32 * 512, 1], [1, 128]]),
                self.IDEN[0:1, 0:1]),
                deps=_mx(tqr, self.fr_pb[3] if h == 0 else None), sig="t")
        tqc = self.v(lambda e: e.tensor_copy(A[:, OH:OH + 8], pb[3][:, 0:8]),
                     deps=_mx(ttc))
        self.fr_pb[3] = tqc
        o_mms = []
        tprev = None
        for h in range(HB):
            tkv = self.mm(pb[5][0:128, 0:128],
                          bass.AP(self.arena, KR + h * 128,
                                  [[32 * 512, 1], [1, 128]]),
                          bass.AP(self.arena, VR + h * 128,
                                  [[32 * 512, 1], [1, 128]]),
                          True, True, deps=_mx(tkr, tv, self.fr_pb[5]))
            ta = self.v(lambda e, hh=h: e.tensor_scalar(
                A[:, 15104:15232], pb[5][:, 0:128],
                bhal[:, 8 + hh: 9 + hh], None, ALU.mult),
                deps=_mx(tkv, tbh, self.F("p3hs")))
            tb = self.v(lambda e, hh=h: e.scalar_tensor_tensor(
                A[:, 15232:15360], s16[:, hh * 128:(hh + 1) * 128],
                bhal[:, hh: hh + 1], A[:, 15104:15232],
                ALU.mult, ALU.add), deps=_mx(ta, deps))
            self.fr_pb[5] = tb
            self.S("p3hs", tb)
            tmo = self.mm(pb[4][0:1, (h % 4) * 128:(h % 4) * 128 + 128],
                          A[:, OH + h: OH + h + 1], A[:, 15232:15360],
                          True, True,
                          deps=_mx(tb, tqc, self.fr_pb[4]))
            tevo = self.v(lambda e, hh=h:
                          e.tensor_copy(A[0:1, 15360 + hh * 128:
                                          15360 + (hh + 1) * 128],
                                        pb[4][0:1,
                                              (hh % 4) * 128:
                                              (hh % 4) * 128 + 128]),
                          deps=_mx(tmo, self.F("p3or")))
            self.fr_pb[4] = tevo
            o_mms.append(tevo)
        tst = self.dma(self.oTM[2050:2051, :], A[0:1, 15360:16384],
                       deps=_mx(o_mms))
        self.S("p3or", tst)
        self.S("p3oh", tst)
        return tst

    def _halo_left(self, sIn, deps):
        A, pb = self.arena, self.pb
        QR, OH = 11376, 14448
        tq = self.dma(A[0:1, QR:QR + 1024], self.qsb[1:2, :],
                      deps=_mx(deps, self.F("p3qr")), lane="w")
        trc_ = self.dma(self.scr2[0:1, 0:512], self.dh["ropec"][1:2, :],
                        deps=_mx(self.F("p3rc")))
        trs_ = self.dma(self.scr2[0:1, 512:1024], self.dh["ropes"][1:2, :],
                        deps=_mx(self.F("p3rs")))
        tqr = self._rope_row(QR, trc_, trs_, _mx(tq))
        self.S("p3rc", tqr)
        self.S("p3rs", tqr)
        tqc = []
        for h in range(HB):
            ttc = self.em.op("tensor", lambda e, hh=h: e.transpose(
                pb[3][0:128, hh:hh + 1],
                bass.AP(self.arena, QR + hh * 128, [[32 * 512, 1], [1, 128]]),
                self.IDEN[0:1, 0:1]),
                deps=_mx(tqr, self.fr_pb[3] if h == 0 else None), sig="t")
        tqc = self.v(lambda e: e.tensor_copy(A[:, OH:OH + 8], pb[3][:, 0:8]),
                     deps=_mx(ttc, self.F("p3qc")))
        self.fr_pb[3] = tqc
        o_mms = []
        for h in range(HB):
            tmo = self.mm(pb[4][0:1, (h % 4) * 128:(h % 4) * 128 + 128],
                          A[:, OH + h: OH + h + 1],
                          sIn[:, h * 128:(h + 1) * 128],
                          True, True,
                          deps=_mx(deps, tqc, self.fr_pb[4]))
            tevo = self.v(lambda e, hh=h:
                          e.tensor_copy(A[0:1, 15360 + hh * 128:
                                          15360 + (hh + 1) * 128],
                                        pb[4][0:1,
                                              (hh % 4) * 128:
                                              (hh % 4) * 128 + 128]),
                          deps=_mx(tmo, self.F("p3or")))
            self.fr_pb[4] = tevo
            o_mms.append(tevo)
        tst = self.dma(self.oTM[1:2, :], A[0:1, 15360:16384],
                       deps=_mx(o_mms))
        self.S("p3or", tst)
        return tst

    def _rope_row(self, base, tcs, tsn, deps):
        scr, scr2 = self.scr, self.scr2

        def sl(off):
            return bass.AP(self.arena, base + off,
                           [[32 * 512, 1], [128, 8], [2, 64]])

        def sc(off):
            return bass.AP(scr, off, [[4 * 512, 1], [64, 8], [1, 64]])
        cosap = bass.AP(scr2, 0, [[2 * 512, 1], [64, 8], [1, 64]])
        sinap = bass.AP(scr2, 512, [[2 * 512, 1], [64, 8], [1, 64]])
        t1 = self.v(lambda e: e.tensor_tensor(sc(0), sl(0), cosap, ALU.mult),
                    deps=_mx(deps, tcs, self.F("scr")))
        t2 = self.v(lambda e: e.tensor_tensor(sc(512), sl(1), sinap, ALU.mult),
                    deps=_mx(deps, tsn))
        t3 = self.v(lambda e: e.tensor_tensor(sc(1024), sl(1), cosap,
                                              ALU.mult), deps=_mx(deps, tcs))
        t4 = self.v(lambda e: e.tensor_tensor(sc(1536), sl(0), sinap,
                                              ALU.mult), deps=_mx(deps, tsn))
        t5 = self.v(lambda e: e.tensor_tensor(sl(0), sc(0), sc(512),
                                              ALU.subtract),
                    deps=_mx(t1, t2, t4))
        t6 = self.v(lambda e: e.tensor_tensor(sl(1), sc(1024), sc(1536),
                                              ALU.add), deps=_mx(t3, t5))
        self.S("scr", t6)
        return t6

    def output(self):
        toks = []
        for cc in range(8):
            toks.append(self.dma(
                self.outh[cc * 128:(cc + 1) * 128, :],
                self.xres[:, cc * W + 2: cc * W + 2 + TOK],
                deps=_mx(self.tok_stream), sig="dout"))
        self.em.op("sync", lambda e: e.nop(), deps=_mx(toks))

    def build(self):
        from contextlib import ExitStack
        es = ExitStack()
        with es:
            self.alloc_sbuf(es)
            self.phase0()
            if self.max_phase >= 1:
                self.phase1()
            if self.max_phase >= 2:
                self.emit_ffn("p2")
            if self.max_phase >= 3:
                self.phase3()
            if self.max_phase >= 4:
                self.emit_ffn("p4")
            self.output()
            final = [(n, v) for n, v in self.em.sem_val.items()
                     if n.startswith(("d", "w", "y"))]
            self.em.op("sync", lambda e: e.nop(), deps=final)
            with self.nc.Block() as block:
                self.em.emit(block)
        mybir.codegen_inst_isa_subclasses(self.nc)
        return self.nc


def build_program(max_phase=MAX_PHASE, groups=None):
    nc = bass.Bass("TRN2", target_bir_lowering=False)
    return Builder(nc, max_phase, groups).build()


def kernel(**inputs):
    per_core_x, cst, rope_c, rope_s, masks = _host_prep(inputs)
    nc = build_program()
    in_maps = []
    for c in range(NCORE):
        m = {"x": per_core_x[c], "ropec": rope_c[c], "ropes": rope_s[c],
             "m_odd": masks[c][0], "m_even": masks[c][1]}
        m.update(cst)
        in_maps.append(m)
    res = run_bass_kernel_spmd(nc, in_maps, core_ids=list(range(NCORE)))
    out = np.zeros((B, L, D), dtype=np.float32)
    for c in range(NCORE):
        b, half = c // 2, c % 2
        out[b, half * TOK:(half + 1) * TOK, :] = \
            np.asarray(res.results[c]["out"]).T
    return out
